# version 27
# baseline (speedup 1.0000x reference)
import contextlib
import numpy as np
import concourse.bass as bass
import concourse.mybir as mybir
from concourse.bass_utils import run_bass_kernel_spmd

F32 = mybir.dt.float32
BF16 = mybir.dt.bfloat16
I32 = mybir.dt.int32
AF = mybir.ActivationFunctionType
ALU = mybir.AluOpType

NSEQ = 2
S = 2048
D = 1024
FF = 4096
NDC = D // 128
NTT = S // 128
QB = 512
NQB = S // QB
NKT = S // 128
NPAIR = 8
TB = 512
NTB = S // TB
NHM = FF // 128
SCALE = 64.0 ** -0.5
EXPB = 12.0
MC = 1.5 * 2.0 ** 23


class Ctr:
    def __init__(self, es, nc, name):
        self.sem = es.enter_context(nc.semaphore(name))
        self.n = 0


class Prog:
    def __init__(self):
        self.events = []

    def on(self, eng, fn):
        self.events.append((eng, fn))

    def run(self, name, eng):
        for e, fn in self.events:
            if e == name:
                fn(eng)


def build():
    nc = bass.Bass("TRN2", target_bir_lowering=False, dynamic_dma_scratch_size=16384)
    x = nc.dram_tensor("x", [NSEQ, S, D], F32, kind="ExternalInput").ap()
    wqkv = nc.dram_tensor("wqkv", [D, 3 * D], F32, kind="ExternalInput").ap()
    bqkv = nc.dram_tensor("bqkv", [3 * D], F32, kind="ExternalInput").ap()
    wproj = nc.dram_tensor("wproj", [D, D], F32, kind="ExternalInput").ap()
    bproj = nc.dram_tensor("bproj", [D], F32, kind="ExternalInput").ap()
    w1 = nc.dram_tensor("w1", [D, FF], F32, kind="ExternalInput").ap()
    b1 = nc.dram_tensor("b1", [FF], F32, kind="ExternalInput").ap()
    w2 = nc.dram_tensor("w2", [FF, D], F32, kind="ExternalInput").ap()
    b2 = nc.dram_tensor("b2", [D], F32, kind="ExternalInput").ap()
    ident = nc.dram_tensor("ident", [128, 128], F32, kind="ExternalInput").ap()
    out = nc.dram_tensor("out", [NSEQ, S, D], F32, kind="ExternalOutput").ap()
    x1s = nc.dram_tensor("x1s", [NSEQ, S, D], F32, kind="Internal").ap()

    es = contextlib.ExitStack()
    p = Prog()
    with es:
        identb = es.enter_context(nc.sbuf_tensor("identb", [128, 128], BF16))
        onesc = es.enter_context(nc.sbuf_tensor("onesc", [128, 64], BF16))
        negB = es.enter_context(nc.sbuf_tensor("negB", [128, 1], F32))
        neg4 = es.enter_context(nc.sbuf_tensor("neg4", [128, 1], F32))
        neg375 = es.enter_context(nc.sbuf_tensor("neg375", [128, 1], F32))
        cmc = es.enter_context(nc.sbuf_tensor("cmc", [128, 1], F32))
        p155 = es.enter_context(nc.sbuf_tensor("p155", [128, 1], F32))
        phf = es.enter_context(nc.sbuf_tensor("phf", [128, 1], F32))
        nmc256 = es.enter_context(nc.sbuf_tensor("nmc256", [128, 1], F32))
        bqt = es.enter_context(nc.sbuf_tensor("bqt", [128, 16], F32))
        b1t = es.enter_context(nc.sbuf_tensor("b1t", [128, 32], F32))
        b256 = es.enter_context(nc.sbuf_tensor("b256", [128, 32], F32))
        b2bc = es.enter_context(nc.sbuf_tensor("b2bc", [128, 1024], BF16))
        ln2d = es.enter_context(nc.sbuf_tensor("ln2d", [128, 2, 18], F32))
        pA = es.enter_context(nc.psum_tensor("pA", [128, 2048], F32))
        pAT = es.enter_context(nc.psum_tensor("pAT", [128, 512], F32))
        pBT = es.enter_context(nc.psum_tensor("pBT", [128, 512], F32))
        pRB = es.enter_context(nc.psum_tensor("pRB", [128, 512], F32))
        pT = es.enter_context(nc.psum_tensor("pT", [128, 1024], BF16))

        s_x = [Ctr(es, nc, f"s_x{i}") for i in range(4)]
        c_lc = Ctr(es, nc, "c_lc")
        c_lp = Ctr(es, nc, "c_lp")
        c_lw = [Ctr(es, nc, "c_lw0"), Ctr(es, nc, "c_lw1")]
        c_lb = Ctr(es, nc, "c_lb")
        c_pe = Ctr(es, nc, "c_pe")
        c_v = Ctr(es, nc, "c_v")
        c_s = Ctr(es, nc, "c_s")
        c_sh = Ctr(es, nc, "c_sh")
        c_g = Ctr(es, nc, "c_g")
        c_stx = Ctr(es, nc, "c_stx")
        c_st = Ctr(es, nc, "c_st")


        esA = contextlib.ExitStack()
        bvbc = esA.enter_context(nc.sbuf_tensor("bvbc", [128, 1024], F32))
        bpbc = esA.enter_context(nc.sbuf_tensor("bpbc", [128, 1024], F32))
        rbsb = esA.enter_context(nc.sbuf_tensor("rbsb", [128, 1024], BF16))
        uAB = esA.enter_context(nc.sbuf_tensor("uAB", [128, 2, 512], F32))
        wstr = esA.enter_context(nc.sbuf_tensor("wstr", [128, 2, NDC, 512], BF16))
        wprojb = esA.enter_context(nc.sbuf_tensor("wprojb", [128, NDC, D], BF16))
        xnfm = esA.enter_context(nc.sbuf_tensor("xnfm", [128, NDC, S], BF16))
        qfm = esA.enter_context(nc.sbuf_tensor("qfm", [128, NDC, S], BF16))
        kfm = esA.enter_context(nc.sbuf_tensor("kfm", [128, NDC, S], BF16))
        vtm = esA.enter_context(nc.sbuf_tensor("vtm", [128, NKT, 16 * 65], BF16))
        PT = esA.enter_context(nc.sbuf_tensor("PTb", [128, 2, 1024], BF16))
        attn = esA.enter_context(nc.sbuf_tensor("attn", [128, NDC, QB], BF16))
        stgB = esA.enter_context(nc.sbuf_tensor("stgB", [128, 512], BF16))
        xstream = esA.enter_context(nc.sbuf_tensor("xstream", [128, 2, 1024], F32))
        xntm = esA.enter_context(nc.sbuf_tensor("xntm", [128, 2, 1024], BF16))
        lnS = esA.enter_context(nc.sbuf_tensor("lnS", [128, 16, 4], F32))
        x1t = esA.enter_context(nc.sbuf_tensor("x1t", [128, 1024], F32))

        def ev(g):
            g.dma_start(out=identb[:], in_=ident).then_inc(c_lc.sem, 16)
            g.dma_start(out=bqt[:], in_=bqkv[0:2048].rearrange("(c q) -> q c", q=128),
                        allow_slow_non_contiguous=True).then_inc(c_lc.sem, 16)
            g.dma_start(out=b1t[:], in_=b1.rearrange("(c q) -> q c", q=128),
                        allow_slow_non_contiguous=True).then_inc(c_lc.sem, 16)
            g.dma_start(out=bvbc[:], in_=bqkv[2048:3072].partition_broadcast(128)).then_inc(c_lc.sem, 16)
            g.dma_start(out=bpbc[:], in_=bproj.partition_broadcast(128)).then_inc(c_lc.sem, 16)
            g.dma_start(out=b2bc[:], in_=b2.partition_broadcast(128)).then_inc(c_lc.sem, 16)
        p.on("gpsimd", ev)
        c_lc.n += 96
        ldw_consts = c_lc.n

        def ev(v, w=ldw_consts):
            v.memset(onesc[:], 1.0)
            v.memset(negB[:], -EXPB)
            v.memset(neg4[:], -4.0)
            v.memset(neg375[:], -3.75)
            v.memset(cmc[:], MC)
            v.memset(p155[:], 15.5)
            v.memset(phf[:], 0.5)
            v.memset(nmc256[:], -49152.0)
            v.wait_ge(c_lc.sem, w)
            v.tensor_scalar(b256[:], b1t[:], 256.0, None, ALU.mult)
            v.tensor_scalar(b2bc[:], b2bc[:], 256.0, None, ALU.mult)
            v.drain()
            v.engine_nop().then_inc(c_v.sem, 1)
        p.on("vector", ev)
        c_v.n += 1
        ldw_consts_v = c_v.n

        def ev(g):
            for c in range(NDC):
                g.dma_start(out=wprojb[:, c, :],
                            in_=wproj[c * 128:(c + 1) * 128, :]).then_inc(c_lp.sem, 16)
        p.on("gpsimd", ev)
        c_lp.n += 16 * NDC
        ldw_proj = c_lp.n

        def ev(v):
            v.memset(vtm[:].rearrange("q t (h j) -> q t h j", j=65)[:, :, :, 64:65], 1.0)
            v.drain()
            v.engine_nop().then_inc(c_v.sem, 1)
        p.on("vector", ev)
        c_v.n += 1
        v_vones = c_v.n

        xg = {}
        last_attn_ev = None
        proj_pe_last = None
        pa_free_v = None
        last_x1_store = None
        x1_store_vals = {}
        proj_pa_guard = {}
        wload = {}
        wmm = {}

        xn1s1 = [None]
        ln1b_prev_ev = [None]
        ln1_lnS_guard = [None]
        ln1_tr_prev = {}
        xn_ev = {}

        def ln1c_stats(sq, tt):
            par = tt % 2

            def ev(sy, sq=sq, tt=tt, par=par, g=xg.get(par)):
                if g is not None:
                    sy.wait_ge(c_s.sem if g[0] == "s" else c_v.sem, g[1])
                sy.dma_start(out=xstream[:, par, :],
                             in_=x[sq, tt * 128:(tt + 1) * 128, :]
                             ).then_inc(s_x[par].sem, 16)
            p.on("sync", ev)
            s_x[par].n += 16
            ldv = s_x[par].n

            def ev(v, tt=tt, par=par, w=ldv,
                   ag=(ln1_lnS_guard[0] if tt == 0 else None)):
                v.wait_ge(s_x[par].sem, w)
                if ag is not None:
                    v.wait_ge(c_s.sem, ag)
                v.bn_stats(ln2d[:, par, 0:6], xstream[:, par, 0:512])
                v.bn_stats(ln2d[:, par, 6:12], xstream[:, par, 512:1024])
                v.drain()
                v.bn_aggr(lnS[:, tt, 0:2], ln2d[:, par, 0:12])
                v.drain()
                v.engine_nop().then_inc(c_v.sem, 1)
            p.on("vector", ev)
            c_v.n += 1
            xg[par] = ("v", c_v.n)

        def ln1c_finish(first=False):
            def ev(v, w=c_v.n):
                v.wait_ge(c_v.sem, w)
                v.tensor_scalar(lnS[:, :, 2:3], lnS[:, :, 1:2], 1e-5, None, ALU.add)
                v.drain()
                v.engine_nop().then_inc(c_v.sem, 1)
            p.on("vector", ev)
            c_v.n += 1

            def ev(sc, w=c_v.n, first=first):
                sc.wait_ge(c_v.sem, w)
                if first:
                    sc.wait_ge(c_lc.sem, ldw_consts)
                sc.drain()
                sc.activation(lnS[:, :, 3:4], lnS[:, :, 2:3],
                              AF.Sqrt).then_inc(c_s.sem, 1)
            p.on("scalar", ev)
            c_s.n += 1

            def ev(v, w=c_s.n):
                v.wait_ge(c_s.sem, w)
                v.reciprocal(lnS[:, :, 1:2], lnS[:, :, 3:4])
                v.drain()
                v.scalar_tensor_tensor(lnS[:, :, 2:3], lnS[:, :, 0:1], -1.0,
                                       lnS[:, :, 1:2], ALU.mult, ALU.mult)
                v.drain()
                v.engine_nop().then_inc(c_v.sem, 1)
            p.on("vector", ev)
            c_v.n += 1
            return c_v.n

        def ln1c_tile(sq, pi, ready):
            par = pi % 2

            def ev(sy, sq=sq, pi=pi, par=par, g=xg.get(par)):
                if g is not None:
                    sy.wait_ge(c_s.sem if g[0] == "s" else c_v.sem, g[1])
                sy.dma_start(out=xstream[:, par, :],
                             in_=x[sq, pi * 128:(pi + 1) * 128, :]
                             ).then_inc(s_x[par].sem, 16)
            p.on("sync", ev)
            s_x[par].n += 16
            ldv = s_x[par].n

            def ev(sc, pi=pi, par=par, w=ldv, rg=(ready if pi == 0 else None),
                   tg=ln1_tr_prev.get(par)):
                sc.wait_ge(s_x[par].sem, w)
                if rg is not None:
                    sc.wait_ge(c_v.sem, rg)
                if tg is not None:
                    sc.wait_ge(c_pe.sem, tg)
                sc.drain()
                sc.activation(xntm[:, par, :], xstream[:, par, :], AF.Identity,
                              bias=lnS[:, pi, 2:3],
                              scale=lnS[:, pi, 1:2]).then_inc(c_s.sem, 1)
            p.on("scalar", ev)
            c_s.n += 1
            xg[par] = ("s", c_s.n)
            apv = c_s.n
            ln1_lnS_guard[0] = c_s.n

            def ev(t, par=par, w=apv, g=ln1b_prev_ev[0]):
                t.wait_ge(c_s.sem, w)
                if g is not None:
                    t.wait_ge(c_v.sem, g)
                for c in range(NDC):
                    ins = t.transpose(pT[:, c * 128:(c + 1) * 128],
                                      xntm[:, par, c * 128:(c + 1) * 128], identb[:])
                ins.then_inc(c_pe.sem, 1)
            p.on("tensor", ev)
            c_pe.n += 1
            ln1_tr_prev[par] = c_pe.n

            def ev(v, pi=pi, w=c_pe.n):
                v.wait_ge(c_pe.sem, w)
                v.tensor_copy(xnfm[:, :, pi * 128:(pi + 1) * 128],
                              pT[:].rearrange("q (c w) -> q c w", w=128))
                v.drain()
                v.engine_nop().then_inc(c_v.sem, 1)
            p.on("vector", ev)
            c_v.n += 1
            ln1b_prev_ev[0] = c_v.n
            return c_v.n

        for s in range(NSEQ):
            prev_evac = {}
            lnv = {}

            def ln_L(tt):
                def ev(sy, s=s, tt=tt, g=xg.get(tt % 2)):
                    if g is not None:
                        sy.wait_ge(c_s.sem if g[0] == "s" else c_v.sem, g[1])
                    sy.dma_start(out=xstream[:, tt % 2, :],
                                 in_=x[s, tt * 128:(tt + 1) * 128, :]).then_inc(s_x[tt % 2].sem, 16)
                p.on("sync", ev)
                s_x[tt % 2].n += 16
                lnv[("L", tt)] = s_x[tt % 2].n

            def ln_V1(tt):
                par = tt % 2

                def ev(v, tt=tt, par=par, w=lnv[("L", tt)]):
                    v.wait_ge(s_x[par].sem, w)
                    v.bn_stats(ln2d[:, par, 0:6], xstream[:, par, 0:512])
                    v.bn_stats(ln2d[:, par, 6:12], xstream[:, par, 512:1024])
                    v.drain()
                    v.bn_aggr(ln2d[:, par, 12:14], ln2d[:, par, 0:12])
                    v.drain()
                    v.tensor_scalar(ln2d[:, par, 14:15], ln2d[:, par, 13:14], 1e-5, None, ALU.add)
                    v.drain()
                    v.engine_nop().then_inc(c_v.sem, 1)
                p.on("vector", ev)
                c_v.n += 1
                lnv[("V1", tt)] = c_v.n

            def ln_S(tt):
                par = tt % 2

                def ev(sc, par=par, w=lnv[("V1", tt)]):
                    sc.wait_ge(c_v.sem, w)
                    sc.activation(ln2d[:, par, 15:16], ln2d[:, par, 14:15],
                                  AF.Sqrt).then_inc(c_s.sem, 1)
                p.on("scalar", ev)
                c_s.n += 1
                lnv[("S", tt)] = c_s.n

            def ln_V2(tt):
                par = tt % 2

                def ev(v, tt=tt, par=par, w=lnv[("S", tt)]):
                    v.wait_ge(c_s.sem, w)
                    v.reciprocal(ln2d[:, par, 16:17], ln2d[:, par, 15:16])
                    v.drain()
                    v.tensor_scalar(ln2d[:, par, 17:18], ln2d[:, par, 12:13],
                                    ln2d[:, par, 16:17], -1.0, ALU.mult, ALU.mult)
                    v.drain()
                    v.engine_nop().then_inc(c_v.sem, 1)
                p.on("vector", ev)
                c_v.n += 1
                lnv[("V2", tt)] = c_v.n

                def ev(sc, tt=tt, par=par, w=c_v.n):
                    sc.wait_ge(c_v.sem, w)
                    sc.activation(xntm[:, par, :], xstream[:, par, :], AF.Identity,
                                  bias=ln2d[:, par, 17:18],
                                  scale=ln2d[:, par, 16:17]).then_inc(c_s.sem, 1)
                p.on("scalar", ev)
                c_s.n += 1
                lnv[("A", tt)] = c_s.n
                xg[tt % 2] = ("s", c_s.n)

            def ln_T(tt):
                par = tt % 2

                def ev(t, par=par, w=lnv[("A", tt)], g=prev_evac.get(tt - 1),
                       first=(s == 0 and tt == 0)):
                    t.wait_ge(c_s.sem, w)
                    if g is not None:
                        t.wait_ge(c_v.sem, g)
                    if first:
                        t.wait_ge(c_lc.sem, ldw_consts)
                    for c in range(NDC):
                        ins = t.transpose(pT[:, c * 128:(c + 1) * 128],
                                          xntm[:, par, c * 128:(c + 1) * 128], identb[:])
                    ins.then_inc(c_pe.sem, 1)
                p.on("tensor", ev)
                c_pe.n += 1
                lnv[("T", tt)] = c_pe.n

            def ln_EV(tt):
                def ev(v, tt=tt, w=lnv[("T", tt)]):
                    v.wait_ge(c_pe.sem, w)
                    v.tensor_copy(xnfm[:, :, tt * 128:(tt + 1) * 128],
                                  pT[:].rearrange("q (c w) -> q c w", w=128))
                    v.drain()
                    v.engine_nop().then_inc(c_v.sem, 1)
                p.on("vector", ev)
                c_v.n += 1
                prev_evac[tt] = c_v.n

            if s == 0:
                for tt in range(NTT):
                    ln1c_stats(0, tt)
                rdy = ln1c_finish(first=True)
                for tt in range(NTT):
                    xn_ev[tt] = ln1c_tile(0, tt, rdy)
                xn_done = c_v.n
            else:
                xn_done = xn1s1[0]

            def emit_wload(g_, prev_user):
                col0 = g_ * 512

                def ev(gp, s=s, g_=g_, col0=col0, mmg=prev_user):
                    if mmg is not None:
                        gp.wait_ge(c_pe.sem, mmg)
                    for dc in range(NDC):
                        gp.dma_start(out=wstr[:, g_ % 2, dc, :],
                                     in_=wqkv[dc * 128:(dc + 1) * 128, col0:col0 + 512]
                                     ).then_inc(c_lw[g_ % 2].sem, 16)
                p.on("gpsimd", ev)
                c_lw[g_ % 2].n += 16 * NDC
                wload[(s, g_)] = c_lw[g_ % 2].n

            emit_wload(0, wmm.get((s - 1, 4)))
            emit_wload(1, wmm.get((s - 1, 5)))

            psg = {}
            psl = [pA[:, 0:512], pA[:, 512:1024], pA[:, 1024:1536], pA[:, 1536:2048]]
            gi = 0
            for g_ in range(4):
                for mq in range(4):
                    mc = g_ * 4 + mq
                    for tb in range(4):
                        bi = gi % 4

                        def ev(t, s=s, g_=g_, mc=mc, mq=mq, tb=tb, bi=bi,
                               pg=psg.get(gi - 4), first=(gi == 0), early=(gi < 4),
                               xw=(xn_ev[4 * tb + 3] if (s == 0 and mq == 0) else
                                   (xn_done if gi == 0 else None)),
                               wl=wload[(s, g_)], pafv=pa_free_v):
                            if pg is not None:
                                t.wait_ge(c_s.sem if pg[0] == "s" else c_v.sem, pg[1])
                            if xw is not None:
                                t.wait_ge(c_v.sem, xw)
                            if mq == 0 and tb == 0:
                                t.wait_ge(c_lw[g_ % 2].sem, wl)
                            if early and pafv is not None:
                                t.wait_ge(c_v.sem, pafv)
                            for dc in range(NDC):
                                ins = t.matmul(psl[bi],
                                               wstr[:, g_ % 2, dc, mq * 128:(mq + 1) * 128],
                                               xnfm[:, dc, tb * 512:(tb + 1) * 512],
                                               start=(dc == 0), stop=(dc == NDC - 1))
                            ins.then_inc(c_pe.sem, 1)
                        p.on("tensor", ev)
                        c_pe.n += 1
                        wmm[(s, g_)] = c_pe.n

                        def ev(sc, mc=mc, bi=bi, tb=tb, w=c_pe.n):
                            sc.wait_ge(c_pe.sem, w)
                            dst = (qfm if mc < 8 else kfm)[:, mc % 8, tb * 512:(tb + 1) * 512]
                            sc.activation(dst, psl[bi], AF.Identity,
                                          bias=bqt[:, mc:mc + 1],
                                          scale=1.0).then_inc(c_s.sem, 1)
                        p.on("scalar", ev)
                        c_s.n += 1
                        psg[gi] = ("s", c_s.n)
                        gi += 1
                emit_wload(g_ + 2, wmm[(s, g_)])
            for g_ in range(4, 6):
                nb = g_ - 4
                for tt in range(NTT):
                    bi = gi % 4

                    def ev(t, s=s, g_=g_, nb=nb, tt=tt, bi=bi, pg=psg.get(gi - 4),
                           vo=v_vones, wl=wload[(s, g_)],
                           xw=(xn_ev[tt] if s == 0 else None)):
                        if pg is not None:
                            t.wait_ge(c_s.sem if pg[0] == "s" else c_v.sem, pg[1])
                        if xw is not None:
                            t.wait_ge(c_v.sem, xw)
                        if tt == 0:
                            t.wait_ge(c_lw[g_ % 2].sem, wl)
                            if nb == 0:
                                t.wait_ge(c_v.sem, vo)
                        for dc in range(NDC):
                            ins = t.matmul(psl[bi], xnfm[:, dc, tt * 128:(tt + 1) * 128],
                                           wstr[:, g_ % 2, dc, :],
                                           start=(dc == 0), stop=(dc == NDC - 1))
                        ins.then_inc(c_pe.sem, 1)
                    p.on("tensor", ev)
                    c_pe.n += 1
                    wmm[(s, g_)] = c_pe.n

                    def ev(v, nb=nb, tt=tt, bi=bi, w=c_pe.n):
                        v.wait_ge(c_pe.sem, w)
                        dst = vtm[:, tt, :].rearrange("q (h j) -> q h j", j=65)[:, nb * 8:(nb + 1) * 8, 0:64]
                        src = psl[bi].rearrange("q (h j) -> q h j", j=64)
                        add = bvbc[:, nb * 512:(nb + 1) * 512].rearrange("q (h j) -> q h j", j=64)
                        v.tensor_tensor(dst, src, add, ALU.add)
                        v.drain()
                        v.engine_nop().then_inc(c_v.sem, 1)
                    p.on("vector", ev)
                    c_v.n += 1
                    psg[gi] = ("v", c_v.n)
                    gi += 1
            qkv_done = c_v.n
            qkv_done_s = c_s.n

            if s == 0:
                for tt in range(NTT):
                    ln1c_stats(1, tt)
                ln1b_ready = ln1c_finish()

                def ln1b_tile(pi):
                    xn1s1[0] = ln1c_tile(1, pi, ln1b_ready)

            pending2 = [None]

            def emit_tail1(pv_done):
                def ev(v, w=pv_done, ug=att_guards.get("uread"),
                       bg=att_guards.get("bcastB")):
                    v.wait_ge(c_pe.sem, w)
                    if ug is not None:
                        v.wait_ge(c_v.sem, ug)
                    v.tensor_copy(uAB[0:65, 0, :], pAT[0:65, :])
                    v.tensor_copy(uAB[0:65, 1, :], pBT[0:65, :])
                    v.drain()
                    v.engine_nop().then_inc(c_v.sem, 1)
                p.on("vector", ev)
                c_v.n += 1
                ucp = c_v.n

                def ev(v, bg=att_guards.get("bcastB")):
                    if bg is not None:
                        v.wait_ge(c_pe.sem, bg)
                    with nc.allow_low_precision(reason="softmax denom recip as bf16 bcast operand"):
                        v.reciprocal(rbsb[64:65, 0:512],
                                     uAB[64:65, 0, :]).then_inc(c_v.sem, 1)
                p.on("vector", ev)
                c_v.n += 1
                rvA = c_v.n

                def ev(v):
                    with nc.allow_low_precision(reason="softmax denom recip as bf16 bcast operand"):
                        v.reciprocal(rbsb[64:65, 512:1024],
                                     uAB[64:65, 1, :]).then_inc(c_v.sem, 1)
                p.on("vector", ev)
                c_v.n += 1
                return ucp, (rvA, c_v.n)

            def emit_bc(pr, rval, half):
                def ev(t, w=rval, half=half, rg=att_guards.get("mul" + ("A" if half else "B"))):
                    t.wait_ge(c_v.sem, w)
                    if rg is not None:
                        t.wait_ge(c_v.sem, rg)
                    t.matmul(pRB[0:64, :], onesc[64:65, :],
                             rbsb[64:65, half * 512:(half + 1) * 512],
                             start=True, stop=True, tile_position=(64, 0),
                             skip_group_check=True).then_inc(c_pe.sem, 1)
                p.on("tensor", ev)
                c_pe.n += 1
                if half == 1:
                    att_guards["bcastB"] = c_pe.n
                return c_pe.n

            def emit_mul(pr, bcv, half):
                def ev(v, pr=pr, w=bcv, half=half, shg=att_guards.get("shuffle"),
                       pg=proj_pe_last):
                    v.wait_ge(c_pe.sem, w)
                    if shg is not None:
                        v.wait_ge(c_sh.sem, shg)
                    if pg is not None:
                        v.wait_ge(c_pe.sem, pg)
                    if half == 0:
                        v.tensor_tensor(attn[0:64, pr, :], uAB[0:64, 0, :],
                                        pRB[0:64, :], ALU.mult)
                    else:
                        v.tensor_tensor(stgB[0:64, :], uAB[0:64, 1, :],
                                        pRB[0:64, :], ALU.mult)
                    v.drain()
                    v.engine_nop().then_inc(c_v.sem, 1)
                p.on("vector", ev)
                c_v.n += 1
                att_guards["mul" + ("A" if half == 0 else "B")] = c_v.n
                if half == 1:
                    att_guards["uread"] = c_v.n
                return c_v.n

            def emit_tail2_stage(st, pr, rval):
                if st == 0:
                    att_guards["_bc0"] = emit_bc(pr, rval[0], 0)
                elif st == 1:
                    att_guards["_mul0"] = emit_mul(pr, att_guards["_bc0"], 0)
                elif st == 2:
                    att_guards["_bc1"] = emit_bc(pr, max(rval[1], att_guards["_mul0"]), 1)
                else:
                    ev_val = emit_mul(pr, att_guards["_bc1"], 1)
                    emit_shuffle(pr, ev_val)

            def emit_tail2(pr, rval):
                for st in range(4):
                    emit_tail2_stage(st, pr, rval)

            def emit_shuffle(pr, ev_val):
                def ev(g, pr=pr, w=ev_val, pg=proj_pe_last):
                    g.wait_ge(c_v.sem, w)
                    if pg is not None:
                        g.wait_ge(c_pe.sem, pg)
                    g.dma_start(out=attn[64:128, pr, :],
                                in_=stgB[0:64, :]).then_inc(c_sh.sem, 16)
                p.on("gpsimd", ev)
                c_sh.n += 16
                att_guards["shuffle"] = c_sh.n

            att_guards = {}
            last_attn_ev_box = [last_attn_ev]
            for qb in range(NQB):
                for pr in range(NPAIR):
                    if s == 0 and qb * NPAIR + pr < NTT:
                        ln1b_tile(qb * NPAIR + pr)
                    exp_vals = {}
                    pv_vals = {}

                    def emit_st(kt, first_extra):
                        sb = kt % 2

                        def ev(t, qb=qb, pr=pr, kt=kt, sb=sb, eg=exp_vals.get(kt - 2),
                               first=first_extra, qd=qkv_done, qds=qkv_done_s,
                               pafv=(proj_pa_guard.get(kt % 2) if first_extra else None)):
                            if eg is not None:
                                t.wait_ge(c_s.sem, eg)
                            if first:
                                t.wait_ge(c_v.sem, qd)
                                t.wait_ge(c_s.sem, qds)
                            if pafv is not None:
                                t.wait_ge(c_v.sem, pafv)
                            off = sb * 1024
                            ko = kt * 128
                            qs = qb * QB
                            t.matmul(pA[:, off:off + 512], kfm[0:64, pr, ko:ko + 128],
                                     qfm[0:64, pr, qs:qs + 512], start=True, stop=True,
                                     skip_group_check=True)
                            t.matmul(pA[:, off + 512:off + 1024],
                                     kfm[64:128, pr, ko:ko + 128],
                                     qfm[64:128, pr, qs:qs + 512], start=True, stop=True,
                                     skip_group_check=True).then_inc(c_pe.sem, 1)
                        p.on("tensor", ev)
                        c_pe.n += 1

                        def ev(sc, kt=kt, sb=sb, w=c_pe.n):
                            sc.wait_ge(c_pe.sem, w)
                            off = sb * 1024
                            sc.activation(PT[:, kt % 2, :], pA[:, off:off + 1024], AF.Exp,
                                          bias=negB[:, 0:1], scale=SCALE).then_inc(c_s.sem, 1)
                        p.on("scalar", ev)
                        c_s.n += 1
                        exp_vals[kt] = c_s.n

                    def emit_pv(kt):
                        evg = last_attn_ev_box[0] if kt == 0 else None

                        def ev(t, pr=pr, kt=kt, w=exp_vals[kt], evg=evg):
                            t.wait_ge(c_s.sem, w)
                            if evg is not None:
                                t.wait_ge(c_v.sem, evg)
                            lhs = vtm[:, kt, :].rearrange("q (h j) -> q h j", j=65)
                            t.matmul(pAT[0:65, :], lhs[:, 2 * pr, :], PT[:, kt % 2, 0:512],
                                     start=(kt == 0), stop=(kt == NKT - 1), skip_group_check=True)
                            t.matmul(pBT[0:65, :], lhs[:, 2 * pr + 1, :], PT[:, kt % 2, 512:1024],
                                     start=(kt == 0), stop=(kt == NKT - 1),
                                     skip_group_check=True).then_inc(c_pe.sem, 1)
                        p.on("tensor", ev)
                        c_pe.n += 1
                        pv_vals[kt] = c_pe.n

                    emit_st(0, pr == 0)
                    emit_st(1, pr == 0)
                    emit_pv(0)
                    tmarks = {4: 0, 6: 1, 8: 2, 10: 3}
                    for kt in range(2, NKT):
                        emit_st(kt, False)
                        if kt in tmarks and pending2[0] is not None:
                            emit_tail2_stage(tmarks[kt], *pending2[0])
                            if tmarks[kt] == 3:
                                pending2[0] = None
                        emit_pv(kt - 1)
                    emit_pv(NKT - 1)
                    ucp, rval = emit_tail1(c_pe.n)
                    last_attn_ev_box[0] = ucp
                    pending2[0] = (pr, rval)
                if pending2[0] is not None:
                    emit_tail2(*pending2[0])
                    pending2[0] = None

                last_sh = att_guards.get("shuffle")
                exp_all = c_s.n
                ev_all = c_v.n
                proj_ev_vals = []
                for tt in range(4):
                    def ev(sy, s=s, qb=qb, tt=tt, g=xg.get(tt % 2)):
                        if g is not None:
                            sy.wait_ge(c_s.sem if g[0] == "s" else c_v.sem, g[1])
                        sy.dma_start(out=xstream[:, tt % 2, :],
                                     in_=x[s, qb * QB + tt * 128:qb * QB + (tt + 1) * 128, :]
                                     ).then_inc(s_x[tt % 2].sem, 16)
                    p.on("sync", ev)
                    s_x[tt % 2].n += 16
                    ldval = s_x[tt % 2].n

                    def ev(t, qb=qb, tt=tt, w=last_sh, e=ev_all, xa=exp_all,
                           first=(tt == 0 and qb == 0 and s == 0),
                           pg=proj_pa_guard.get(tt % 2)):
                        t.wait_ge(c_sh.sem, w)
                        t.wait_ge(c_v.sem, e)
                        t.wait_ge(c_s.sem, xa)
                        if pg is not None:
                            t.wait_ge(c_v.sem, pg)
                        if first:
                            t.wait_ge(c_lp.sem, ldw_proj)
                        for nb in range(2):
                            for dc in range(NDC):
                                ins = t.matmul(
                                    pA[:, (tt % 2) * 1024 + nb * 512:(tt % 2) * 1024 + (nb + 1) * 512],
                                    attn[:, dc, tt * 128:(tt + 1) * 128],
                                    wprojb[:, dc, nb * 512:(nb + 1) * 512],
                                    start=(dc == 0), stop=(dc == NDC - 1))
                        ins.then_inc(c_pe.sem, 1)
                    p.on("tensor", ev)
                    c_pe.n += 1

                    def ev(v, tt=tt, w=c_pe.n, lv=ldval, sg=last_x1_store):
                        v.wait_ge(c_pe.sem, w)
                        v.wait_ge(s_x[tt % 2].sem, lv)
                        if sg is not None:
                            v.wait_ge(c_stx.sem, sg)
                        base = (tt % 2) * 1024
                        v.tensor_tensor(x1t[:], pA[:, base:base + 1024], bpbc[:], ALU.add)
                        v.drain()
                        v.tensor_tensor(x1t[:], x1t[:], xstream[:, tt % 2, :], ALU.add)
                        v.drain()
                        v.engine_nop().then_inc(c_v.sem, 1)
                    p.on("vector", ev)
                    c_v.n += 1
                    xg[tt % 2] = ("v", c_v.n)
                    proj_pa_guard[tt % 2] = c_v.n
                    proj_ev_vals.append(c_v.n)

                    def ev(sy, s=s, qb=qb, tt=tt, w=c_v.n):
                        sy.wait_ge(c_v.sem, w)
                        sy.dma_start(out=x1s[s, qb * QB + tt * 128:qb * QB + (tt + 1) * 128, :],
                                     in_=x1t[:]).then_inc(c_stx.sem, 16)
                    p.on("sync", ev)
                    c_stx.n += 16
                    last_x1_store = c_stx.n
                    x1_store_vals[(s, qb, tt)] = c_stx.n
                proj_pe_last = c_pe.n
                pa_free_v = proj_ev_vals[-1]

        last_pe_A = c_pe.n
        last_v_A = c_v.n
        last_s_A = c_s.n
        stx_A = c_stx.n
        esA.close()

        esB = contextlib.ExitStack()
        w1s = esB.enter_context(nc.sbuf_tensor("w1s", [128, 2, NDC, 256], BF16))
        w2b = esB.enter_context(nc.sbuf_tensor("w2b", [128, NHM, D], BF16))
        gbuf = esB.enter_context(nc.sbuf_tensor("gbuf", [128, 2, NHM, TB], BF16))
        xn2 = esB.enter_context(nc.sbuf_tensor("xn2", [128, 2, NDC, TB], BF16))
        x1rd = esB.enter_context(nc.sbuf_tensor("x1rd", [128, 1024], F32))
        x1rd2 = esB.enter_context(nc.sbuf_tensor("x1rd2", [128, 2, 1024], F32))
        xn2tm = esB.enter_context(nc.sbuf_tensor("xn2tm", [128, 1024], BF16))
        ft = esB.enter_context(nc.sbuf_tensor("ft", [128, 4, 512], F32))
        fki = esB.enter_context(nc.sbuf_tensor("fki", [128, 4, 512], F32))
        fu5 = esB.enter_context(nc.sbuf_tensor("fu5", [128, 3, 512], F32))
        fji = esB.enter_context(nc.sbuf_tensor("fji", [128, 2, 512], F32))
        fgl = esB.enter_context(nc.sbuf_tensor("fgl", [128, 2, 512], F32))
        w2scr = esB.enter_context(nc.sbuf_tensor("w2scr", [128, 2, 512], F32))
        outt = esB.enter_context(nc.sbuf_tensor("outt", [128, 2, 512], F32))

        def ev(g, a=last_pe_A, b=last_v_A, c=last_s_A, d=stx_A):
            g.wait_ge(c_pe.sem, a)
            g.wait_ge(c_v.sem, b)
            g.wait_ge(c_s.sem, c)
            g.wait_ge(c_stx.sem, d)
            for c_ in range(NHM):
                g.dma_start(out=w2b[:, c_, :], in_=w2[c_ * 128:(c_ + 1) * 128, :]).then_inc(c_lb.sem, 16)
        p.on("gpsimd", ev)
        c_lb.n += 16 * NHM
        ldw_B = c_lb.n

        def ev(v, d=stx_A):
            v.wait_ge(c_stx.sem, d)
        p.on("vector", ev)

        def ev(sc, d=stx_A):
            sc.wait_ge(c_stx.sem, d)
        p.on("scalar", ev)

        def ev(t, d=stx_A):
            t.wait_ge(c_stx.sem, d)
        p.on("tensor", ev)

        blocks = [(s, tb) for s in range(NSEQ) for tb in range(NTB)]
        NB = len(blocks)
        w1load = {}
        w1mm = {}
        ln2_apply = {}
        ln2_T_prev = [None]
        ln2_pT_guard = [None]
        xn2_ready = {}
        g_done_of = {}
        w2_last_of = {}
        w2ev_guard = {}
        x2rd_guard = {}
        first_w2 = [True]

        def w1_group_load(bi, grp):
            sl = grp % 2
            prev = w1mm.get((bi, grp - 2)) if grp >= 2 else w1mm.get((bi - 1, 14 + grp))

            def ev(gp, grp=grp, sl=sl, prev=prev):
                if prev is not None:
                    gp.wait_ge(c_pe.sem, prev)
                for dc in range(NDC):
                    gp.dma_start(out=w1s[:, sl, dc, :],
                                 in_=w1[dc * 128:(dc + 1) * 128, grp * 256:(grp + 1) * 256]
                                 ).then_inc(c_lw[sl].sem, 16)
            p.on("gpsimd", ev)
            c_lw[sl].n += 16 * NDC
            w1load[(bi, grp)] = c_lw[sl].n

        ln2_ld = {}
        ln2_v1 = {}
        ln2_sq = {}
        ln2_rc = {}
        ln2_ap = {}
        ln2_tr = {}

        ln2_ap_last = [None]

        def ln2_L(bi, tt):
            s, tb = blocks[bi]
            par = tt % 2
            first = (bi == 0 and tt == 0)

            def ev(sy, s=s, tb=tb, tt=tt, stv=x1_store_vals[(s, tb, tt)],
                   g=ln2_ap_last[0], first=first,
                   a=last_pe_A, b=last_v_A, c=last_s_A):
                if first:
                    sy.wait_ge(c_pe.sem, a)
                    sy.wait_ge(c_v.sem, b)
                    sy.wait_ge(c_s.sem, c)
                sy.wait_ge(c_stx.sem, stv)
                if g is not None:
                    sy.wait_ge(c_s.sem, g)
                sy.dma_start(out=x1rd[:, :],
                             in_=x1s[s, tb * TB + tt * 128:tb * TB + (tt + 1) * 128, :]
                             ).then_inc(s_x[par].sem, 16)
            p.on("sync", ev)
            s_x[par].n += 16
            ln2_ld[(bi, tt)] = s_x[par].n

        def ln2_A(bi, tt):
            par = tt % 2

            def ev(v, par=par, w=ln2_ld[(bi, tt)]):
                v.wait_ge(s_x[par].sem, w)
                v.bn_stats(ln2d[:, par, 0:6], x1rd[:, 0:512])
                v.bn_stats(ln2d[:, par, 6:12], x1rd[:, 512:1024])
                v.drain()
                v.bn_aggr(ln2d[:, par, 12:14], ln2d[:, par, 0:12])
                v.drain()
                v.tensor_scalar(ln2d[:, par, 14:15], ln2d[:, par, 13:14], 1e-5,
                                None, ALU.add).then_inc(c_v.sem, 1)
            p.on("vector", ev)
            c_v.n += 1
            ln2_v1[(bi, tt)] = c_v.n

            def ev(sc, par=par, w=c_v.n):
                sc.wait_ge(c_v.sem, w)
                sc.drain()
                sc.activation(ln2d[:, par, 15:16], ln2d[:, par, 14:15],
                              AF.Sqrt).then_inc(c_s.sem, 1)
            p.on("scalar", ev)
            c_s.n += 1
            ln2_sq[(bi, tt)] = c_s.n

        def ln2_B(bi, tt):
            par = tt % 2

            def ev(v, par=par, w=ln2_sq[(bi, tt)]):
                v.wait_ge(c_s.sem, w)
                v.reciprocal(ln2d[:, par, 16:17], ln2d[:, par, 15:16])
                v.drain()
                v.tensor_scalar(ln2d[:, par, 17:18], ln2d[:, par, 12:13],
                                ln2d[:, par, 16:17], -1.0, ALU.mult,
                                ALU.mult).then_inc(c_v.sem, 1)
            p.on("vector", ev)
            c_v.n += 1
            ln2_rc[(bi, tt)] = c_v.n

            def ev(sc, par=par, w=c_v.n, tg=ln2_T_prev[0]):
                sc.wait_ge(c_v.sem, w)
                if tg is not None:
                    sc.wait_ge(c_pe.sem, tg)
                sc.drain()
                sc.activation(xn2tm[:], x1rd[:, :], AF.Identity,
                              bias=ln2d[:, par, 17:18],
                              scale=ln2d[:, par, 16:17]).then_inc(c_s.sem, 1)
            p.on("scalar", ev)
            c_s.n += 1
            ln2_ap[(bi, tt)] = c_s.n
            ln2_ap_last[0] = c_s.n

        def ln2_C1(bi, tt):
            def ev(t, w=ln2_ap[(bi, tt)], g=ln2_pT_guard[0]):
                t.wait_ge(c_s.sem, w)
                if g is not None:
                    t.wait_ge(c_v.sem, g)
                for c in range(NDC):
                    ins = t.transpose(pT[:, c * 128:(c + 1) * 128],
                                      xn2tm[:, c * 128:(c + 1) * 128], identb[:])
                ins.then_inc(c_pe.sem, 1)
            p.on("tensor", ev)
            c_pe.n += 1
            ln2_T_prev[0] = c_pe.n
            ln2_tr[(bi, tt)] = c_pe.n

        def ln2_C2(bi, tt):
            def ev(v, bi=bi, tt=tt, w=ln2_tr[(bi, tt)]):
                v.wait_ge(c_pe.sem, w)
                v.tensor_copy(xn2[:, bi % 2, :, tt * 128:(tt + 1) * 128],
                              pT[:].rearrange("q (c w) -> q c w", w=128)
                              ).then_inc(c_v.sem, 1)
            p.on("vector", ev)
            c_v.n += 1
            ln2_pT_guard[0] = c_v.n
            return c_v.n

        e1_hist = {}
        e5_hist = {}
        e6_hist = {}
        e8_hist = {}

        def act_E012(h):
            bi, hm = divmod(h, 32)
            hb = hm % 2
            i = h % 3
            i4 = h % 4
            grp = hm // 2
            sl = grp % 2

            def ev(t, bi=bi, hm=hm, hb=hb, sl=sl,
                   xd=xn2_ready[bi], wl=(w1load[(bi, grp)] if hm % 2 == 0 else None),
                   ag=e1_hist.get(h - 2)):
                t.wait_ge(c_v.sem, xd)
                if wl is not None:
                    t.wait_ge(c_lw[sl].sem, wl)
                if ag is not None:
                    t.wait_ge(c_s.sem, ag)
                for dc in range(NDC):
                    ins = t.matmul(pA[:, hb * 512:(hb + 1) * 512],
                                   w1s[:, sl, dc, (hm % 2) * 128:(hm % 2 + 1) * 128],
                                   xn2[:, bi % 2, dc, :],
                                   start=(dc == 0), stop=(dc == NDC - 1))
                ins.then_inc(c_pe.sem, 1)
            p.on("tensor", ev)
            c_pe.n += 1
            w1mm[(bi, grp)] = c_pe.n
            w1_pe = c_pe.n

            def ev(sc, hm=hm, hb=hb, i4=i4, w=w1_pe, vg=e6_hist.get(h - 4)):
                sc.wait_ge(c_pe.sem, w)
                if vg is not None:
                    sc.wait_ge(c_v.sem, vg)
                sc.activation(ft[:, i4, :], pA[:, hb * 512:(hb + 1) * 512], AF.Identity,
                              bias=b256[:, hm:hm + 1], scale=256.0).then_inc(c_s.sem, 1)
            p.on("scalar", ev)
            c_s.n += 1
            e1_hist[h] = c_s.n
            sid_done = c_s.n

            def ev(v, i=i, i4=i4, w=sid_done, sg=e5_hist.get(h - 3)):
                v.wait_ge(c_s.sem, w)
                if sg is not None:
                    v.wait_ge(c_s.sem, sg)
                v.tensor_scalar(fki[:, i4, :], ft[:, i4, :], -1152.5, 1152.5,
                                ALU.max, ALU.min)
                v.drain()
                v.tensor_scalar(ft[:, i4, :], fki[:, i4, :], MC, -MC,
                                ALU.add, ALU.add).then_inc(c_v.sem, 1)
            p.on("vector", ev)
            c_v.n += 1
            e1_hist[(h, "va")] = c_v.n

        def act_E34(h):
            i = h % 3
            i4 = h % 4

            def ev(sc, i=i, i4=i4, w=e1_hist[(h, "va")]):
                sc.wait_ge(c_v.sem, w)
                sc.activation(fu5[:, i, :], ft[:, i4, :], AF.Identity,
                              bias=p155[:, 0:1], scale=1.0 / 64.0).then_inc(c_s.sem, 1)
            p.on("scalar", ev)
            c_s.n += 1
            u5_done = c_s.n

            def ev(v, i=i, i4=i4, w=u5_done):
                v.wait_ge(c_s.sem, w)
                v.tensor_scalar(fki[:, i4, :], fu5[:, i, :], MC, -MC, ALU.add, ALU.add)
                v.drain()
                v.scalar_tensor_tensor(ft[:, i4, :], fki[:, i4, :], -1.0, fu5[:, i, :],
                                       ALU.mult, ALU.add).then_inc(c_v.sem, 1)
            p.on("vector", ev)
            c_v.n += 1
            e1_hist[(h, "vb")] = c_v.n

        def act_E56(h):
            i = h % 3
            i4 = h % 4
            j = h % 2

            def ev(sc, i=i, i4=i4, j=j, w=e1_hist[(h, "vb")], fg=e6_hist.get(h - 2)):
                sc.wait_ge(c_v.sem, w)
                if fg is not None:
                    sc.wait_ge(c_v.sem, fg)
                sc.drain()
                sc.add(fu5[:, i, :], ft[:, i4, :], phf[:, 0:1])
                sc.activation(fgl[:, j, :], fki[:, i4, :], AF.Gelu, bias=neg4[:, 0:1],
                              scale=0.25)
                sc.activation(fji[:, j, :], fki[:, i4, :], AF.Gelu, bias=neg375[:, 0:1],
                              scale=0.25).then_inc(c_s.sem, 1)
            p.on("scalar", ev)
            c_s.n += 1
            e5_hist[h] = c_s.n
            s3_done = c_s.n

            def ev(v, i=i, i4=i4, j=j, w=s3_done):
                v.wait_ge(c_s.sem, w)
                v.tensor_sub(ft[:, i4, :], fji[:, j, :], fgl[:, j, :])
                v.drain()
                v.scalar_tensor_tensor(fki[:, i4, :], ft[:, i4, :], 256.0, fu5[:, i, :],
                                       ALU.mult, ALU.mult)
                v.drain()
                v.scalar_tensor_tensor(fu5[:, i, :], fgl[:, j, :], 256.0, fki[:, i4, :],
                                       ALU.mult, ALU.add).then_inc(c_v.sem, 1)
            p.on("vector", ev)
            c_v.n += 1
            e6_hist[h] = c_v.n

        def act_E78(h):
            bi, hm = divmod(h, 32)
            i = h % 3
            j = h % 2

            def ev(sc, bi=bi, hm=hm, i=i, j=j, w=e6_hist[h],
                   gg=w2_last_of.get(bi - 2)):
                sc.wait_ge(c_v.sem, w)
                if gg is not None:
                    sc.wait_ge(c_pe.sem, gg)
                sc.add(fji[:, j, :], fu5[:, i, :], cmc[:, 0:1])
                sc.drain()
                sc.activation(gbuf[:, bi % 2, hm, :], fji[:, j, :], AF.Identity,
                              bias=nmc256[:, 0:1],
                              scale=1.0 / 256.0).then_inc(c_s.sem, 1)
            p.on("scalar", ev)
            c_s.n += 1
            e8_hist[h] = c_s.n

        w2_pe_of = {}

        def w2_mm(bi, j):
            s, tb = blocks[bi]
            tt, nb = j // 2, j % 2
            ob = j % 2
            par = tt % 2
            if nb == 0:
                def ev(sy, s=s, tb=tb, tt=tt, par=par, g=x2rd_guard.get(par)):
                    if g is not None:
                        sy.wait_ge(c_v.sem, g)
                    sy.dma_start(out=x1rd2[:, par, :],
                                 in_=x1s[s, tb * TB + tt * 128:tb * TB + (tt + 1) * 128, :]
                                 ).then_inc(s_x[2 + par].sem, 16)
                p.on("sync", ev)
                s_x[2 + par].n += 16

            def ev(t, bi=bi, tt=tt, nb=nb, ob=ob, gd=g_done_of[bi],
                   g=w2ev_guard.get(ob), fw=first_w2[0]):
                if fw:
                    t.wait_ge(c_lb.sem, ldw_B)
                t.wait_ge(c_s.sem, gd)
                if g is not None:
                    t.wait_ge(c_v.sem, g)
                for hm in range(NHM):
                    ins = t.matmul(pA[:, 1024 + ob * 512:1024 + (ob + 1) * 512],
                                   gbuf[:, bi % 2, hm, tt * 128:(tt + 1) * 128],
                                   w2b[:, hm, nb * 512:(nb + 1) * 512],
                                   start=(hm == 0), stop=(hm == NHM - 1))
                ins.then_inc(c_pe.sem, 1)
            p.on("tensor", ev)
            c_pe.n += 1
            first_w2[0] = False
            w2_pe_of[(bi, j)] = c_pe.n
            w2_last_of[bi] = c_pe.n

        def w2_ev(bi, j):
            s, tb = blocks[bi]
            tt, nb = j // 2, j % 2
            ob = j % 2
            par = tt % 2
            x2ld = s_x[2 + par].n

            def ev(v, nb=nb, ob=ob, par=par, w=w2_pe_of[(bi, j)], x2=x2ld,
                   og=(c_st.n if c_st.n > 0 else None)):
                v.wait_ge(c_pe.sem, w)
                v.wait_ge(s_x[2 + par].sem, x2)
                if og is not None:
                    v.wait_ge(c_st.sem, og)
                v.scalar_tensor_tensor(w2scr[:, 0, :],
                                       pA[:, 1024 + ob * 512:1024 + (ob + 1) * 512],
                                       256.0, b2bc[:, nb * 512:(nb + 1) * 512],
                                       ALU.mult, ALU.add)
                v.drain()
                v.tensor_scalar(w2scr[:, 1, :], w2scr[:, 0, :], MC, -MC, ALU.add, ALU.add)
                v.drain()
                v.scalar_tensor_tensor(outt[:, ob, :], w2scr[:, 1, :], 1.0 / 256.0,
                                       x1rd2[:, par, nb * 512:(nb + 1) * 512],
                                       ALU.mult, ALU.add).then_inc(c_v.sem, 1)
            p.on("vector", ev)
            c_v.n += 1
            w2ev_guard[ob] = c_v.n
            if nb == 1:
                x2rd_guard[par] = c_v.n

            def ev(sy, s=s, tb=tb, tt=tt, nb=nb, ob=ob, w=c_v.n):
                sy.wait_ge(c_v.sem, w)
                sy.dma_start(out=out[s, tb * TB + tt * 128:tb * TB + (tt + 1) * 128,
                                     nb * 512:(nb + 1) * 512],
                             in_=outt[:, ob, :]).then_inc(c_st.sem, 16)
            p.on("sync", ev)
            c_st.n += 16

        w1_group_load(0, 0)
        w1_group_load(0, 1)
        for tt in range(4):
            ln2_L(0, tt)
            ln2_A(0, tt)
            ln2_B(0, tt)
            ln2_C1(0, tt)
            xn2_ready[0] = ln2_C2(0, tt)

        w2mm_marks = (3, 6, 9, 12, 15, 18, 21, 24)
        w2ev_marks = (5, 8, 11, 14, 17, 20, 23, 26)
        NG = NB * NHM
        for g in range(NG + 3):
            if g < NG:
                bi, hm = divmod(g, NHM)
                if bi > 0 and hm in w2mm_marks:
                    w2_mm(bi - 1, w2mm_marks.index(hm))
                if bi > 0 and hm in w2ev_marks:
                    w2_ev(bi - 1, w2ev_marks.index(hm))
                if hm % 2 == 0 and 2 <= hm // 2 + 1 <= 15:
                    w1_group_load(bi, hm // 2 + 1)
                if hm == 30 and bi + 1 < NB:
                    w1_group_load(bi + 1, 0)
                act_E012(g)
                if hm == 31 and bi + 1 < NB:
                    w1_group_load(bi + 1, 1)
                if bi + 1 < NB and 3 <= hm <= 20:
                    tt, st = divmod(hm - 3, 4)
                    if st == 0 and hm >= 3 and tt < 4:
                        ln2_L(bi + 1, tt)
                    elif st == 2:
                        ln2_A(bi + 1, tt)
                    elif st == 3:
                        ln2_B(bi + 1, tt)
                    elif st == 1 and tt >= 1:
                        ln2_C1(bi + 1, tt - 1)
                        xn2_ready[bi + 1] = ln2_C2(bi + 1, tt - 1)
            if g >= 1 and g - 1 < NG:
                act_E34(g - 1)
            if g >= 2 and g - 2 < NG:
                act_E56(g - 2)
            if g >= 3 and g - 3 < NG:
                act_E78(g - 3)
                if (g - 3) % NHM == NHM - 1:
                    g_done_of[(g - 3) // NHM] = c_s.n
        for j in range(8):
            w2_mm(NB - 1, j)
            w2_ev(NB - 1, j)
        st_final = c_st.n

        def ev(sy, w=st_final):
            sy.wait_ge(c_st.sem, w)
        p.on("sync", ev)

        esB.close()

        block = es.enter_context(nc.Block())

        @block.sync
        def _(sync):
            p.run("sync", sync)

        @block.gpsimd
        def _(g):
            p.run("gpsimd", g)

        @block.tensor
        def _(t):
            p.run("tensor", t)

        @block.vector
        def _(v):
            p.run("vector", v)

        @block.scalar
        def _(sc):
            p.run("scalar", sc)

    return nc


def kernel(L0_attn_x, L0_attn_w_qkv, L0_attn_b_qkv, L0_attn_w_proj, L0_attn_b_proj,
           L0_ffn_w1, L0_ffn_b1, L0_ffn_w2, L0_ffn_b2):
    xs = np.ascontiguousarray(np.asarray(L0_attn_x, dtype=np.float32))
    common = {
        "wqkv": np.ascontiguousarray(np.asarray(L0_attn_w_qkv, dtype=np.float32)),
        "bqkv": np.ascontiguousarray(np.asarray(L0_attn_b_qkv, dtype=np.float32)),
        "wproj": np.ascontiguousarray(np.asarray(L0_attn_w_proj, dtype=np.float32)),
        "bproj": np.ascontiguousarray(np.asarray(L0_attn_b_proj, dtype=np.float32)),
        "w1": np.ascontiguousarray(np.asarray(L0_ffn_w1, dtype=np.float32)),
        "b1": np.ascontiguousarray(np.asarray(L0_ffn_b1, dtype=np.float32)),
        "w2": np.ascontiguousarray(np.asarray(L0_ffn_w2, dtype=np.float32)),
        "b2": np.ascontiguousarray(np.asarray(L0_ffn_b2, dtype=np.float32)),
        "ident": np.eye(128, dtype=np.float32),
    }
    nc = build()
    in_maps = []
    for c in range(8):
        m = dict(common)
        m["x"] = np.ascontiguousarray(xs[2 * c:2 * c + 2])
        in_maps.append(m)
    res = run_bass_kernel_spmd(nc, in_maps, core_ids=list(range(8)))
    return np.concatenate([r["out"] for r in res.results], axis=0)


# version 28
# speedup vs baseline: 1.0146x; 1.0146x over previous
import contextlib
import numpy as np
import concourse.bass as bass
import concourse.mybir as mybir
from concourse.bass_utils import run_bass_kernel_spmd

F32 = mybir.dt.float32
BF16 = mybir.dt.bfloat16
I32 = mybir.dt.int32
AF = mybir.ActivationFunctionType
ALU = mybir.AluOpType

NSEQ = 2
S = 2048
D = 1024
FF = 4096
NDC = D // 128
NTT = S // 128
QB = 512
NQB = S // QB
NKT = S // 128
NPAIR = 8
TB = 512
NTB = S // TB
NHM = FF // 128
SCALE = 64.0 ** -0.5
EXPB = 12.0
MC = 1.5 * 2.0 ** 23


class Ctr:
    def __init__(self, es, nc, name):
        self.sem = es.enter_context(nc.semaphore(name))
        self.n = 0


class Prog:
    def __init__(self):
        self.events = []

    def on(self, eng, fn):
        self.events.append((eng, fn))

    def run(self, name, eng):
        for e, fn in self.events:
            if e == name:
                fn(eng)


def build():
    nc = bass.Bass("TRN2", target_bir_lowering=False, dynamic_dma_scratch_size=16384)
    x = nc.dram_tensor("x", [NSEQ, S, D], F32, kind="ExternalInput").ap()
    wqkv = nc.dram_tensor("wqkv", [D, 3 * D], F32, kind="ExternalInput").ap()
    bqkv = nc.dram_tensor("bqkv", [3 * D], F32, kind="ExternalInput").ap()
    wproj = nc.dram_tensor("wproj", [D, D], F32, kind="ExternalInput").ap()
    bproj = nc.dram_tensor("bproj", [D], F32, kind="ExternalInput").ap()
    w1 = nc.dram_tensor("w1", [D, FF], F32, kind="ExternalInput").ap()
    b1 = nc.dram_tensor("b1", [FF], F32, kind="ExternalInput").ap()
    w2 = nc.dram_tensor("w2", [FF, D], F32, kind="ExternalInput").ap()
    b2 = nc.dram_tensor("b2", [D], F32, kind="ExternalInput").ap()
    ident = nc.dram_tensor("ident", [128, 128], F32, kind="ExternalInput").ap()
    out = nc.dram_tensor("out", [NSEQ, S, D], F32, kind="ExternalOutput").ap()
    x1s = nc.dram_tensor("x1s", [NSEQ, S, D], F32, kind="Internal").ap()

    es = contextlib.ExitStack()
    p = Prog()
    with es:
        identb = es.enter_context(nc.sbuf_tensor("identb", [128, 128], BF16))
        onesc = es.enter_context(nc.sbuf_tensor("onesc", [128, 64], BF16))
        negB = es.enter_context(nc.sbuf_tensor("negB", [128, 1], F32))
        neg4 = es.enter_context(nc.sbuf_tensor("neg4", [128, 1], F32))
        neg375 = es.enter_context(nc.sbuf_tensor("neg375", [128, 1], F32))
        cmc = es.enter_context(nc.sbuf_tensor("cmc", [128, 1], F32))
        p155 = es.enter_context(nc.sbuf_tensor("p155", [128, 1], F32))
        phf = es.enter_context(nc.sbuf_tensor("phf", [128, 1], F32))
        nmc256 = es.enter_context(nc.sbuf_tensor("nmc256", [128, 1], F32))
        bqt = es.enter_context(nc.sbuf_tensor("bqt", [128, 16], F32))
        b1t = es.enter_context(nc.sbuf_tensor("b1t", [128, 32], F32))
        b256 = es.enter_context(nc.sbuf_tensor("b256", [128, 32], F32))
        b2bc = es.enter_context(nc.sbuf_tensor("b2bc", [128, 1024], BF16))
        ln2d = es.enter_context(nc.sbuf_tensor("ln2d", [128, 2, 18], F32))
        pA = es.enter_context(nc.psum_tensor("pA", [128, 2048], F32))
        pAT = es.enter_context(nc.psum_tensor("pAT", [128, 512], F32))
        pBT = es.enter_context(nc.psum_tensor("pBT", [128, 512], F32))
        pRB = es.enter_context(nc.psum_tensor("pRB", [128, 512], F32))
        pT = es.enter_context(nc.psum_tensor("pT", [128, 1024], BF16))

        s_x = [Ctr(es, nc, f"s_x{i}") for i in range(4)]
        c_lc = Ctr(es, nc, "c_lc")
        c_lp = Ctr(es, nc, "c_lp")
        c_lw = [Ctr(es, nc, "c_lw0"), Ctr(es, nc, "c_lw1")]
        c_lb = Ctr(es, nc, "c_lb")
        c_pe = Ctr(es, nc, "c_pe")
        c_v = Ctr(es, nc, "c_v")
        c_s = Ctr(es, nc, "c_s")
        c_sh = Ctr(es, nc, "c_sh")
        c_g = Ctr(es, nc, "c_g")
        c_stx = Ctr(es, nc, "c_stx")
        c_st = Ctr(es, nc, "c_st")


        esA = contextlib.ExitStack()
        bvbc = esA.enter_context(nc.sbuf_tensor("bvbc", [128, 1024], F32))
        bpbc = esA.enter_context(nc.sbuf_tensor("bpbc", [128, 1024], F32))
        rbsb = esA.enter_context(nc.sbuf_tensor("rbsb", [128, 1024], BF16))
        uAB = esA.enter_context(nc.sbuf_tensor("uAB", [128, 2, 512], F32))
        wstr = esA.enter_context(nc.sbuf_tensor("wstr", [128, 2, NDC, 512], BF16))
        wprojb = esA.enter_context(nc.sbuf_tensor("wprojb", [128, NDC, D], BF16))
        xnfm = esA.enter_context(nc.sbuf_tensor("xnfm", [128, NDC, S], BF16))
        qfm = esA.enter_context(nc.sbuf_tensor("qfm", [128, NDC, S], BF16))
        kfm = esA.enter_context(nc.sbuf_tensor("kfm", [128, NDC, S], BF16))
        vtm = esA.enter_context(nc.sbuf_tensor("vtm", [128, NKT, 16 * 65], BF16))
        PT = esA.enter_context(nc.sbuf_tensor("PTb", [128, 2, 1024], BF16))
        attn = esA.enter_context(nc.sbuf_tensor("attn", [128, NDC, QB], BF16))
        stgB = esA.enter_context(nc.sbuf_tensor("stgB", [128, 512], BF16))
        xstream = esA.enter_context(nc.sbuf_tensor("xstream", [128, 2, 1024], F32))
        xntm = esA.enter_context(nc.sbuf_tensor("xntm", [128, 2, 1024], BF16))
        lnS = esA.enter_context(nc.sbuf_tensor("lnS", [128, 16, 4], F32))
        x1t = esA.enter_context(nc.sbuf_tensor("x1t", [128, 1024], F32))

        def ev(g):
            g.dma_start(out=identb[:], in_=ident).then_inc(c_lc.sem, 16)
            g.dma_start(out=bqt[:], in_=bqkv[0:2048].rearrange("(c q) -> q c", q=128),
                        allow_slow_non_contiguous=True).then_inc(c_lc.sem, 16)
            g.dma_start(out=b1t[:], in_=b1.rearrange("(c q) -> q c", q=128),
                        allow_slow_non_contiguous=True).then_inc(c_lc.sem, 16)
            g.dma_start(out=bvbc[:], in_=bqkv[2048:3072].partition_broadcast(128)).then_inc(c_lc.sem, 16)
            g.dma_start(out=bpbc[:], in_=bproj.partition_broadcast(128)).then_inc(c_lc.sem, 16)
            g.dma_start(out=b2bc[:], in_=b2.partition_broadcast(128)).then_inc(c_lc.sem, 16)
        p.on("gpsimd", ev)
        c_lc.n += 96
        ldw_consts = c_lc.n

        def ev(v, w=ldw_consts):
            v.memset(onesc[:], 1.0)
            v.memset(negB[:], -EXPB)
            v.memset(neg4[:], -4.0)
            v.memset(neg375[:], -3.75)
            v.memset(cmc[:], MC)
            v.memset(p155[:], 15.5)
            v.memset(phf[:], 0.5)
            v.memset(nmc256[:], -49152.0)
            v.wait_ge(c_lc.sem, w)
            v.tensor_scalar(b256[:], b1t[:], 256.0, None, ALU.mult)
            v.tensor_scalar(b2bc[:], b2bc[:], 256.0, None, ALU.mult)
            v.drain()
            v.engine_nop().then_inc(c_v.sem, 1)
        p.on("vector", ev)
        c_v.n += 1
        ldw_consts_v = c_v.n

        def ev(g):
            for c in range(NDC):
                g.dma_start(out=wprojb[:, c, :],
                            in_=wproj[c * 128:(c + 1) * 128, :]).then_inc(c_lp.sem, 16)
        p.on("gpsimd", ev)
        c_lp.n += 16 * NDC
        ldw_proj = c_lp.n

        def ev(v):
            v.memset(vtm[:].rearrange("q t (h j) -> q t h j", j=65)[:, :, :, 64:65], 1.0)
            v.drain()
            v.engine_nop().then_inc(c_v.sem, 1)
        p.on("vector", ev)
        c_v.n += 1
        v_vones = c_v.n

        xg = {}
        last_attn_ev = None
        proj_pe_last = None
        pa_free_v = None
        last_x1_store = None
        x1_store_vals = {}
        proj_pa_guard = {}
        wload = {}
        wmm = {}

        xn1s1 = [None]
        ln1b_prev_ev = [None]
        ln1_lnS_guard = [None]
        ln1_tr_prev = {}
        xn_ev = {}

        def ln1c_stats(sq, tt):
            par = tt % 2

            def ev(sy, sq=sq, tt=tt, par=par, g=xg.get(par)):
                if g is not None:
                    sy.wait_ge(c_s.sem if g[0] == "s" else c_v.sem, g[1])
                sy.dma_start(out=xstream[:, par, :],
                             in_=x[sq, tt * 128:(tt + 1) * 128, :]
                             ).then_inc(s_x[par].sem, 16)
            p.on("sync", ev)
            s_x[par].n += 16
            ldv = s_x[par].n

            def ev(v, tt=tt, par=par, w=ldv,
                   ag=(ln1_lnS_guard[0] if tt == 0 else None)):
                v.wait_ge(s_x[par].sem, w)
                if ag is not None:
                    v.wait_ge(c_s.sem, ag)
                v.bn_stats(ln2d[:, par, 0:6], xstream[:, par, 0:512])
                v.bn_stats(ln2d[:, par, 6:12], xstream[:, par, 512:1024])
                v.drain()
                v.bn_aggr(lnS[:, tt, 0:2], ln2d[:, par, 0:12])
                v.drain()
                v.engine_nop().then_inc(c_v.sem, 1)
            p.on("vector", ev)
            c_v.n += 1
            xg[par] = ("v", c_v.n)

        def ln1c_finish(first=False):
            def ev(v, w=c_v.n):
                v.wait_ge(c_v.sem, w)
                v.tensor_scalar(lnS[:, :, 2:3], lnS[:, :, 1:2], 1e-5, None, ALU.add)
                v.drain()
                v.engine_nop().then_inc(c_v.sem, 1)
            p.on("vector", ev)
            c_v.n += 1

            def ev(sc, w=c_v.n, first=first):
                sc.wait_ge(c_v.sem, w)
                if first:
                    sc.wait_ge(c_lc.sem, ldw_consts)
                sc.drain()
                sc.activation(lnS[:, :, 3:4], lnS[:, :, 2:3],
                              AF.Sqrt).then_inc(c_s.sem, 1)
            p.on("scalar", ev)
            c_s.n += 1

            def ev(v, w=c_s.n):
                v.wait_ge(c_s.sem, w)
                v.reciprocal(lnS[:, :, 1:2], lnS[:, :, 3:4])
                v.drain()
                v.scalar_tensor_tensor(lnS[:, :, 2:3], lnS[:, :, 0:1], -1.0,
                                       lnS[:, :, 1:2], ALU.mult, ALU.mult)
                v.drain()
                v.engine_nop().then_inc(c_v.sem, 1)
            p.on("vector", ev)
            c_v.n += 1
            return c_v.n

        def ln1c_tile(sq, pi, ready):
            par = pi % 2

            def ev(sy, sq=sq, pi=pi, par=par, g=xg.get(par)):
                if g is not None:
                    sy.wait_ge(c_s.sem if g[0] == "s" else c_v.sem, g[1])
                sy.dma_start(out=xstream[:, par, :],
                             in_=x[sq, pi * 128:(pi + 1) * 128, :]
                             ).then_inc(s_x[par].sem, 16)
            p.on("sync", ev)
            s_x[par].n += 16
            ldv = s_x[par].n

            def ev(sc, pi=pi, par=par, w=ldv, rg=(ready if pi == 0 else None),
                   tg=ln1_tr_prev.get(par)):
                sc.wait_ge(s_x[par].sem, w)
                if rg is not None:
                    sc.wait_ge(c_v.sem, rg)
                if tg is not None:
                    sc.wait_ge(c_pe.sem, tg)
                sc.drain()
                sc.activation(xntm[:, par, :], xstream[:, par, :], AF.Identity,
                              bias=lnS[:, pi, 2:3],
                              scale=lnS[:, pi, 1:2]).then_inc(c_s.sem, 1)
            p.on("scalar", ev)
            c_s.n += 1
            xg[par] = ("s", c_s.n)
            apv = c_s.n
            ln1_lnS_guard[0] = c_s.n

            def ev(t, par=par, w=apv, g=ln1b_prev_ev[0]):
                t.wait_ge(c_s.sem, w)
                if g is not None:
                    t.wait_ge(c_v.sem, g)
                for c in range(NDC):
                    ins = t.transpose(pT[:, c * 128:(c + 1) * 128],
                                      xntm[:, par, c * 128:(c + 1) * 128], identb[:])
                ins.then_inc(c_pe.sem, 1)
            p.on("tensor", ev)
            c_pe.n += 1
            ln1_tr_prev[par] = c_pe.n

            def ev(v, pi=pi, w=c_pe.n):
                v.wait_ge(c_pe.sem, w)
                v.tensor_copy(xnfm[:, :, pi * 128:(pi + 1) * 128],
                              pT[:].rearrange("q (c w) -> q c w", w=128))
                v.drain()
                v.engine_nop().then_inc(c_v.sem, 1)
            p.on("vector", ev)
            c_v.n += 1
            ln1b_prev_ev[0] = c_v.n
            return c_v.n

        for s in range(NSEQ):
            prev_evac = {}
            lnv = {}

            def ln_L(tt):
                def ev(sy, s=s, tt=tt, g=xg.get(tt % 2)):
                    if g is not None:
                        sy.wait_ge(c_s.sem if g[0] == "s" else c_v.sem, g[1])
                    sy.dma_start(out=xstream[:, tt % 2, :],
                                 in_=x[s, tt * 128:(tt + 1) * 128, :]).then_inc(s_x[tt % 2].sem, 16)
                p.on("sync", ev)
                s_x[tt % 2].n += 16
                lnv[("L", tt)] = s_x[tt % 2].n

            def ln_V1(tt):
                par = tt % 2

                def ev(v, tt=tt, par=par, w=lnv[("L", tt)]):
                    v.wait_ge(s_x[par].sem, w)
                    v.bn_stats(ln2d[:, par, 0:6], xstream[:, par, 0:512])
                    v.bn_stats(ln2d[:, par, 6:12], xstream[:, par, 512:1024])
                    v.drain()
                    v.bn_aggr(ln2d[:, par, 12:14], ln2d[:, par, 0:12])
                    v.drain()
                    v.tensor_scalar(ln2d[:, par, 14:15], ln2d[:, par, 13:14], 1e-5, None, ALU.add)
                    v.drain()
                    v.engine_nop().then_inc(c_v.sem, 1)
                p.on("vector", ev)
                c_v.n += 1
                lnv[("V1", tt)] = c_v.n

            def ln_S(tt):
                par = tt % 2

                def ev(sc, par=par, w=lnv[("V1", tt)]):
                    sc.wait_ge(c_v.sem, w)
                    sc.activation(ln2d[:, par, 15:16], ln2d[:, par, 14:15],
                                  AF.Sqrt).then_inc(c_s.sem, 1)
                p.on("scalar", ev)
                c_s.n += 1
                lnv[("S", tt)] = c_s.n

            def ln_V2(tt):
                par = tt % 2

                def ev(v, tt=tt, par=par, w=lnv[("S", tt)]):
                    v.wait_ge(c_s.sem, w)
                    v.reciprocal(ln2d[:, par, 16:17], ln2d[:, par, 15:16])
                    v.drain()
                    v.tensor_scalar(ln2d[:, par, 17:18], ln2d[:, par, 12:13],
                                    ln2d[:, par, 16:17], -1.0, ALU.mult, ALU.mult)
                    v.drain()
                    v.engine_nop().then_inc(c_v.sem, 1)
                p.on("vector", ev)
                c_v.n += 1
                lnv[("V2", tt)] = c_v.n

                def ev(sc, tt=tt, par=par, w=c_v.n):
                    sc.wait_ge(c_v.sem, w)
                    sc.activation(xntm[:, par, :], xstream[:, par, :], AF.Identity,
                                  bias=ln2d[:, par, 17:18],
                                  scale=ln2d[:, par, 16:17]).then_inc(c_s.sem, 1)
                p.on("scalar", ev)
                c_s.n += 1
                lnv[("A", tt)] = c_s.n
                xg[tt % 2] = ("s", c_s.n)

            def ln_T(tt):
                par = tt % 2

                def ev(t, par=par, w=lnv[("A", tt)], g=prev_evac.get(tt - 1),
                       first=(s == 0 and tt == 0)):
                    t.wait_ge(c_s.sem, w)
                    if g is not None:
                        t.wait_ge(c_v.sem, g)
                    if first:
                        t.wait_ge(c_lc.sem, ldw_consts)
                    for c in range(NDC):
                        ins = t.transpose(pT[:, c * 128:(c + 1) * 128],
                                          xntm[:, par, c * 128:(c + 1) * 128], identb[:])
                    ins.then_inc(c_pe.sem, 1)
                p.on("tensor", ev)
                c_pe.n += 1
                lnv[("T", tt)] = c_pe.n

            def ln_EV(tt):
                def ev(v, tt=tt, w=lnv[("T", tt)]):
                    v.wait_ge(c_pe.sem, w)
                    v.tensor_copy(xnfm[:, :, tt * 128:(tt + 1) * 128],
                                  pT[:].rearrange("q (c w) -> q c w", w=128))
                    v.drain()
                    v.engine_nop().then_inc(c_v.sem, 1)
                p.on("vector", ev)
                c_v.n += 1
                prev_evac[tt] = c_v.n

            if s == 0:
                for tt in range(NTT):
                    ln1c_stats(0, tt)
                rdy = ln1c_finish(first=True)
                for tt in range(NTT):
                    xn_ev[tt] = ln1c_tile(0, tt, rdy)
                xn_done = c_v.n
            else:
                xn_done = xn1s1[0]

            def emit_wload(g_, prev_user):
                col0 = g_ * 512

                def ev(gp, s=s, g_=g_, col0=col0, mmg=prev_user):
                    if mmg is not None:
                        gp.wait_ge(c_pe.sem, mmg)
                    for dc in range(NDC):
                        gp.dma_start(out=wstr[:, g_ % 2, dc, :],
                                     in_=wqkv[dc * 128:(dc + 1) * 128, col0:col0 + 512]
                                     ).then_inc(c_lw[g_ % 2].sem, 16)
                p.on("gpsimd", ev)
                c_lw[g_ % 2].n += 16 * NDC
                wload[(s, g_)] = c_lw[g_ % 2].n

            emit_wload(0, wmm.get((s - 1, 4)))
            emit_wload(1, wmm.get((s - 1, 5)))

            psg = {}
            psl = [pA[:, 0:512], pA[:, 512:1024], pA[:, 1024:1536], pA[:, 1536:2048]]
            gi = 0
            for g_ in range(4):
                for mq in range(4):
                    mc = g_ * 4 + mq
                    for tb in range(4):
                        bi = gi % 4

                        def ev(t, s=s, g_=g_, mc=mc, mq=mq, tb=tb, bi=bi,
                               pg=psg.get(gi - 4), first=(gi == 0), early=(gi < 4),
                               xw=(xn_ev[4 * tb + 3] if (s == 0 and mq == 0) else
                                   (xn_done if gi == 0 else None)),
                               wl=wload[(s, g_)], pafv=pa_free_v):
                            if pg is not None:
                                t.wait_ge(c_s.sem if pg[0] == "s" else c_v.sem, pg[1])
                            if xw is not None:
                                t.wait_ge(c_v.sem, xw)
                            if mq == 0 and tb == 0:
                                t.wait_ge(c_lw[g_ % 2].sem, wl)
                            if early and pafv is not None:
                                t.wait_ge(c_v.sem, pafv)
                            for dc in range(NDC):
                                ins = t.matmul(psl[bi],
                                               wstr[:, g_ % 2, dc, mq * 128:(mq + 1) * 128],
                                               xnfm[:, dc, tb * 512:(tb + 1) * 512],
                                               start=(dc == 0), stop=(dc == NDC - 1))
                            ins.then_inc(c_pe.sem, 1)
                        p.on("tensor", ev)
                        c_pe.n += 1
                        wmm[(s, g_)] = c_pe.n

                        def ev(sc, mc=mc, bi=bi, tb=tb, w=c_pe.n):
                            sc.wait_ge(c_pe.sem, w)
                            dst = (qfm if mc < 8 else kfm)[:, mc % 8, tb * 512:(tb + 1) * 512]
                            sc.activation(dst, psl[bi], AF.Identity,
                                          bias=bqt[:, mc:mc + 1],
                                          scale=1.0).then_inc(c_s.sem, 1)
                        p.on("scalar", ev)
                        c_s.n += 1
                        psg[gi] = ("s", c_s.n)
                        gi += 1
                emit_wload(g_ + 2, wmm[(s, g_)])
            for g_ in range(4, 6):
                nb = g_ - 4
                for tt in range(NTT):
                    bi = gi % 4

                    def ev(t, s=s, g_=g_, nb=nb, tt=tt, bi=bi, pg=psg.get(gi - 4),
                           vo=v_vones, wl=wload[(s, g_)],
                           xw=(xn_ev[tt] if s == 0 else None)):
                        if pg is not None:
                            t.wait_ge(c_s.sem if pg[0] == "s" else c_v.sem, pg[1])
                        if xw is not None:
                            t.wait_ge(c_v.sem, xw)
                        if tt == 0:
                            t.wait_ge(c_lw[g_ % 2].sem, wl)
                            if nb == 0:
                                t.wait_ge(c_v.sem, vo)
                        for dc in range(NDC):
                            ins = t.matmul(psl[bi], xnfm[:, dc, tt * 128:(tt + 1) * 128],
                                           wstr[:, g_ % 2, dc, :],
                                           start=(dc == 0), stop=(dc == NDC - 1))
                        ins.then_inc(c_pe.sem, 1)
                    p.on("tensor", ev)
                    c_pe.n += 1
                    wmm[(s, g_)] = c_pe.n

                    def ev(v, nb=nb, tt=tt, bi=bi, w=c_pe.n):
                        v.wait_ge(c_pe.sem, w)
                        dst = vtm[:, tt, :].rearrange("q (h j) -> q h j", j=65)[:, nb * 8:(nb + 1) * 8, 0:64]
                        src = psl[bi].rearrange("q (h j) -> q h j", j=64)
                        add = bvbc[:, nb * 512:(nb + 1) * 512].rearrange("q (h j) -> q h j", j=64)
                        v.tensor_tensor(dst, src, add, ALU.add)
                        v.drain()
                        v.engine_nop().then_inc(c_v.sem, 1)
                    p.on("vector", ev)
                    c_v.n += 1
                    psg[gi] = ("v", c_v.n)
                    gi += 1
            qkv_done = c_v.n
            qkv_done_s = c_s.n

            if s == 0:
                for tt in range(NTT):
                    ln1c_stats(1, tt)
                ln1b_ready = ln1c_finish()

                def ln1b_tile(pi):
                    xn1s1[0] = ln1c_tile(1, pi, ln1b_ready)

            pending2 = [None]

            def emit_tail1(pv_done):
                def ev(v, w=pv_done, ug=att_guards.get("uread"),
                       bg=att_guards.get("bcastB")):
                    v.wait_ge(c_pe.sem, w)
                    if ug is not None:
                        v.wait_ge(c_v.sem, ug)
                    v.tensor_copy(uAB[0:65, 0, :], pAT[0:65, :])
                    v.tensor_copy(uAB[0:65, 1, :], pBT[0:65, :])
                    v.drain()
                    v.engine_nop().then_inc(c_v.sem, 1)
                p.on("vector", ev)
                c_v.n += 1
                ucp = c_v.n

                def ev(v, bg=att_guards.get("bcastB")):
                    if bg is not None:
                        v.wait_ge(c_pe.sem, bg)
                    with nc.allow_low_precision(reason="softmax denom recip as bf16 bcast operand"):
                        v.reciprocal(rbsb[64:65, 0:512],
                                     uAB[64:65, 0, :]).then_inc(c_v.sem, 1)
                p.on("vector", ev)
                c_v.n += 1
                rvA = c_v.n

                def ev(v):
                    with nc.allow_low_precision(reason="softmax denom recip as bf16 bcast operand"):
                        v.reciprocal(rbsb[64:65, 512:1024],
                                     uAB[64:65, 1, :]).then_inc(c_v.sem, 1)
                p.on("vector", ev)
                c_v.n += 1
                return ucp, (rvA, c_v.n)

            def emit_bc(pr, rval, half):
                def ev(t, w=rval, half=half, rg=att_guards.get("mul" + ("A" if half else "B"))):
                    t.wait_ge(c_v.sem, w)
                    if rg is not None:
                        t.wait_ge(c_v.sem, rg)
                    t.matmul(pRB[0:64, :], onesc[64:65, :],
                             rbsb[64:65, half * 512:(half + 1) * 512],
                             start=True, stop=True, tile_position=(64, 0),
                             skip_group_check=True).then_inc(c_pe.sem, 1)
                p.on("tensor", ev)
                c_pe.n += 1
                if half == 1:
                    att_guards["bcastB"] = c_pe.n
                return c_pe.n

            def emit_mul(pr, bcv, half):
                def ev(v, pr=pr, w=bcv, half=half, shg=att_guards.get("shuffle"),
                       pg=proj_pe_last):
                    v.wait_ge(c_pe.sem, w)
                    if shg is not None:
                        v.wait_ge(c_sh.sem, shg)
                    if pg is not None:
                        v.wait_ge(c_pe.sem, pg)
                    if half == 0:
                        v.tensor_tensor(attn[0:64, pr, :], uAB[0:64, 0, :],
                                        pRB[0:64, :], ALU.mult)
                    else:
                        v.tensor_tensor(stgB[0:64, :], uAB[0:64, 1, :],
                                        pRB[0:64, :], ALU.mult)
                    v.drain()
                    v.engine_nop().then_inc(c_v.sem, 1)
                p.on("vector", ev)
                c_v.n += 1
                att_guards["mul" + ("A" if half == 0 else "B")] = c_v.n
                if half == 1:
                    att_guards["uread"] = c_v.n
                return c_v.n

            def emit_tail2_stage(st, pr, rval):
                if st == 0:
                    att_guards["_bc0"] = emit_bc(pr, rval[0], 0)
                elif st == 1:
                    att_guards["_mul0"] = emit_mul(pr, att_guards["_bc0"], 0)
                elif st == 2:
                    att_guards["_bc1"] = emit_bc(pr, max(rval[1], att_guards["_mul0"]), 1)
                else:
                    ev_val = emit_mul(pr, att_guards["_bc1"], 1)
                    emit_shuffle(pr, ev_val)

            def emit_tail2(pr, rval):
                for st in range(4):
                    emit_tail2_stage(st, pr, rval)

            def emit_shuffle(pr, ev_val):
                def ev(g, pr=pr, w=ev_val, pg=proj_pe_last):
                    g.wait_ge(c_v.sem, w)
                    if pg is not None:
                        g.wait_ge(c_pe.sem, pg)
                    g.dma_start(out=attn[64:128, pr, :],
                                in_=stgB[0:64, :]).then_inc(c_sh.sem, 16)
                p.on("gpsimd", ev)
                c_sh.n += 16
                att_guards["shuffle"] = c_sh.n

            att_guards = {}
            last_attn_ev_box = [last_attn_ev]
            for qb in range(NQB):
                for pr in range(NPAIR):
                    if s == 0 and qb * NPAIR + pr < NTT:
                        ln1b_tile(qb * NPAIR + pr)
                    exp_vals = {}
                    pv_vals = {}

                    def emit_st(kt, first_extra):
                        sb = kt % 2

                        def ev(t, qb=qb, pr=pr, kt=kt, sb=sb, eg=exp_vals.get(kt - 2),
                               first=first_extra, qd=qkv_done, qds=qkv_done_s,
                               pafv=(proj_pa_guard.get(kt % 2) if first_extra else None)):
                            if eg is not None:
                                t.wait_ge(c_s.sem, eg)
                            if first:
                                t.wait_ge(c_v.sem, qd)
                                t.wait_ge(c_s.sem, qds)
                            if pafv is not None:
                                t.wait_ge(c_v.sem, pafv)
                            off = sb * 1024
                            ko = kt * 128
                            qs = qb * QB
                            t.matmul(pA[:, off:off + 512], kfm[0:64, pr, ko:ko + 128],
                                     qfm[0:64, pr, qs:qs + 512], start=True, stop=True,
                                     skip_group_check=True)
                            t.matmul(pA[:, off + 512:off + 1024],
                                     kfm[64:128, pr, ko:ko + 128],
                                     qfm[64:128, pr, qs:qs + 512], start=True, stop=True,
                                     skip_group_check=True).then_inc(c_pe.sem, 1)
                        p.on("tensor", ev)
                        c_pe.n += 1

                        def ev(sc, kt=kt, sb=sb, w=c_pe.n):
                            sc.wait_ge(c_pe.sem, w)
                            off = sb * 1024
                            sc.activation(PT[:, kt % 2, :], pA[:, off:off + 1024], AF.Exp,
                                          bias=negB[:, 0:1], scale=SCALE).then_inc(c_s.sem, 1)
                        p.on("scalar", ev)
                        c_s.n += 1
                        exp_vals[kt] = c_s.n

                    def emit_pv(kt):
                        evg = last_attn_ev_box[0] if kt == 0 else None

                        def ev(t, pr=pr, kt=kt, w=exp_vals[kt], evg=evg):
                            t.wait_ge(c_s.sem, w)
                            if evg is not None:
                                t.wait_ge(c_v.sem, evg)
                            lhs = vtm[:, kt, :].rearrange("q (h j) -> q h j", j=65)
                            t.matmul(pAT[0:65, :], lhs[:, 2 * pr, :], PT[:, kt % 2, 0:512],
                                     start=(kt == 0), stop=(kt == NKT - 1), skip_group_check=True)
                            t.matmul(pBT[0:65, :], lhs[:, 2 * pr + 1, :], PT[:, kt % 2, 512:1024],
                                     start=(kt == 0), stop=(kt == NKT - 1),
                                     skip_group_check=True).then_inc(c_pe.sem, 1)
                        p.on("tensor", ev)
                        c_pe.n += 1
                        pv_vals[kt] = c_pe.n

                    emit_st(0, pr == 0)
                    emit_st(1, pr == 0)
                    emit_pv(0)
                    tmarks = {5: 0, 7: 1, 9: 2, 11: 3}
                    for kt in range(2, NKT):
                        emit_st(kt, False)
                        if kt in tmarks and pending2[0] is not None:
                            emit_tail2_stage(tmarks[kt], *pending2[0])
                            if tmarks[kt] == 3:
                                pending2[0] = None
                        emit_pv(kt - 1)
                    emit_pv(NKT - 1)
                    ucp, rval = emit_tail1(c_pe.n)
                    last_attn_ev_box[0] = ucp
                    pending2[0] = (pr, rval)
                if pending2[0] is not None:
                    emit_tail2(*pending2[0])
                    pending2[0] = None

                last_sh = att_guards.get("shuffle")
                exp_all = c_s.n
                ev_all = c_v.n
                proj_ev_vals = []
                for tt in range(4):
                    def ev(sy, s=s, qb=qb, tt=tt, g=xg.get(tt % 2)):
                        if g is not None:
                            sy.wait_ge(c_s.sem if g[0] == "s" else c_v.sem, g[1])
                        sy.dma_start(out=xstream[:, tt % 2, :],
                                     in_=x[s, qb * QB + tt * 128:qb * QB + (tt + 1) * 128, :]
                                     ).then_inc(s_x[tt % 2].sem, 16)
                    p.on("sync", ev)
                    s_x[tt % 2].n += 16
                    ldval = s_x[tt % 2].n

                    def ev(t, qb=qb, tt=tt, w=last_sh, e=ev_all, xa=exp_all,
                           first=(tt == 0 and qb == 0 and s == 0),
                           pg=proj_pa_guard.get(tt % 2)):
                        t.wait_ge(c_sh.sem, w)
                        t.wait_ge(c_v.sem, e)
                        t.wait_ge(c_s.sem, xa)
                        if pg is not None:
                            t.wait_ge(c_v.sem, pg)
                        if first:
                            t.wait_ge(c_lp.sem, ldw_proj)
                        for nb in range(2):
                            for dc in range(NDC):
                                ins = t.matmul(
                                    pA[:, (tt % 2) * 1024 + nb * 512:(tt % 2) * 1024 + (nb + 1) * 512],
                                    attn[:, dc, tt * 128:(tt + 1) * 128],
                                    wprojb[:, dc, nb * 512:(nb + 1) * 512],
                                    start=(dc == 0), stop=(dc == NDC - 1))
                        ins.then_inc(c_pe.sem, 1)
                    p.on("tensor", ev)
                    c_pe.n += 1

                    def ev(v, tt=tt, w=c_pe.n, lv=ldval, sg=last_x1_store):
                        v.wait_ge(c_pe.sem, w)
                        v.wait_ge(s_x[tt % 2].sem, lv)
                        if sg is not None:
                            v.wait_ge(c_stx.sem, sg)
                        base = (tt % 2) * 1024
                        v.tensor_tensor(x1t[:], pA[:, base:base + 1024], bpbc[:], ALU.add)
                        v.drain()
                        v.tensor_tensor(x1t[:], x1t[:], xstream[:, tt % 2, :], ALU.add)
                        v.drain()
                        v.engine_nop().then_inc(c_v.sem, 1)
                    p.on("vector", ev)
                    c_v.n += 1
                    xg[tt % 2] = ("v", c_v.n)
                    proj_pa_guard[tt % 2] = c_v.n
                    proj_ev_vals.append(c_v.n)

                    def ev(sy, s=s, qb=qb, tt=tt, w=c_v.n):
                        sy.wait_ge(c_v.sem, w)
                        sy.dma_start(out=x1s[s, qb * QB + tt * 128:qb * QB + (tt + 1) * 128, :],
                                     in_=x1t[:]).then_inc(c_stx.sem, 16)
                    p.on("sync", ev)
                    c_stx.n += 16
                    last_x1_store = c_stx.n
                    x1_store_vals[(s, qb, tt)] = c_stx.n
                proj_pe_last = c_pe.n
                pa_free_v = proj_ev_vals[-1]

        last_pe_A = c_pe.n
        last_v_A = c_v.n
        last_s_A = c_s.n
        stx_A = c_stx.n
        esA.close()

        esB = contextlib.ExitStack()
        w1s = esB.enter_context(nc.sbuf_tensor("w1s", [128, 2, NDC, 256], BF16))
        w2b = esB.enter_context(nc.sbuf_tensor("w2b", [128, NHM, D], BF16))
        gbuf = esB.enter_context(nc.sbuf_tensor("gbuf", [128, 2, NHM, TB], BF16))
        xn2 = esB.enter_context(nc.sbuf_tensor("xn2", [128, 2, NDC, TB], BF16))
        x1rd = esB.enter_context(nc.sbuf_tensor("x1rd", [128, 1024], F32))
        x1rd2 = esB.enter_context(nc.sbuf_tensor("x1rd2", [128, 2, 1024], F32))
        xn2tm = esB.enter_context(nc.sbuf_tensor("xn2tm", [128, 1024], BF16))
        ft = esB.enter_context(nc.sbuf_tensor("ft", [128, 4, 512], F32))
        fki = esB.enter_context(nc.sbuf_tensor("fki", [128, 4, 512], F32))
        fu5 = esB.enter_context(nc.sbuf_tensor("fu5", [128, 3, 512], F32))
        fji = esB.enter_context(nc.sbuf_tensor("fji", [128, 2, 512], F32))
        fgl = esB.enter_context(nc.sbuf_tensor("fgl", [128, 2, 512], F32))
        w2scr = esB.enter_context(nc.sbuf_tensor("w2scr", [128, 2, 512], F32))
        outt = esB.enter_context(nc.sbuf_tensor("outt", [128, 2, 512], F32))

        def ev(g, a=last_pe_A, b=last_v_A, c=last_s_A, d=stx_A):
            g.wait_ge(c_pe.sem, a)
            g.wait_ge(c_v.sem, b)
            g.wait_ge(c_s.sem, c)
            g.wait_ge(c_stx.sem, d)
            for c_ in range(NHM):
                g.dma_start(out=w2b[:, c_, :], in_=w2[c_ * 128:(c_ + 1) * 128, :]).then_inc(c_lb.sem, 16)
        p.on("gpsimd", ev)
        c_lb.n += 16 * NHM
        ldw_B = c_lb.n

        def ev(v, d=stx_A):
            v.wait_ge(c_stx.sem, d)
        p.on("vector", ev)

        def ev(sc, d=stx_A):
            sc.wait_ge(c_stx.sem, d)
        p.on("scalar", ev)

        def ev(t, d=stx_A):
            t.wait_ge(c_stx.sem, d)
        p.on("tensor", ev)

        blocks = [(s, tb) for s in range(NSEQ) for tb in range(NTB)]
        NB = len(blocks)
        w1load = {}
        w1mm = {}
        ln2_apply = {}
        ln2_T_prev = [None]
        ln2_pT_guard = [None]
        xn2_ready = {}
        g_done_of = {}
        w2_last_of = {}
        w2ev_guard = {}
        x2rd_guard = {}
        first_w2 = [True]

        def w1_group_load(bi, grp):
            sl = grp % 2
            prev = w1mm.get((bi, grp - 2)) if grp >= 2 else w1mm.get((bi - 1, 14 + grp))

            def ev(gp, grp=grp, sl=sl, prev=prev):
                if prev is not None:
                    gp.wait_ge(c_pe.sem, prev)
                for dc in range(NDC):
                    gp.dma_start(out=w1s[:, sl, dc, :],
                                 in_=w1[dc * 128:(dc + 1) * 128, grp * 256:(grp + 1) * 256]
                                 ).then_inc(c_lw[sl].sem, 16)
            p.on("gpsimd", ev)
            c_lw[sl].n += 16 * NDC
            w1load[(bi, grp)] = c_lw[sl].n

        ln2_ld = {}
        ln2_v1 = {}
        ln2_sq = {}
        ln2_rc = {}
        ln2_ap = {}
        ln2_tr = {}

        ln2_ap_last = [None]

        def ln2_L(bi, tt):
            s, tb = blocks[bi]
            par = tt % 2
            first = (bi == 0 and tt == 0)

            def ev(sy, s=s, tb=tb, tt=tt, stv=x1_store_vals[(s, tb, tt)],
                   g=ln2_ap_last[0], first=first,
                   a=last_pe_A, b=last_v_A, c=last_s_A):
                if first:
                    sy.wait_ge(c_pe.sem, a)
                    sy.wait_ge(c_v.sem, b)
                    sy.wait_ge(c_s.sem, c)
                sy.wait_ge(c_stx.sem, stv)
                if g is not None:
                    sy.wait_ge(c_s.sem, g)
                sy.dma_start(out=x1rd[:, :],
                             in_=x1s[s, tb * TB + tt * 128:tb * TB + (tt + 1) * 128, :]
                             ).then_inc(s_x[par].sem, 16)
            p.on("sync", ev)
            s_x[par].n += 16
            ln2_ld[(bi, tt)] = s_x[par].n

        def ln2_A(bi, tt):
            par = tt % 2

            def ev(v, par=par, w=ln2_ld[(bi, tt)]):
                v.wait_ge(s_x[par].sem, w)
                v.bn_stats(ln2d[:, par, 0:6], x1rd[:, 0:512])
                v.bn_stats(ln2d[:, par, 6:12], x1rd[:, 512:1024])
                v.drain()
                v.bn_aggr(ln2d[:, par, 12:14], ln2d[:, par, 0:12])
                v.drain()
                v.tensor_scalar(ln2d[:, par, 14:15], ln2d[:, par, 13:14], 1e-5,
                                None, ALU.add).then_inc(c_v.sem, 1)
            p.on("vector", ev)
            c_v.n += 1
            ln2_v1[(bi, tt)] = c_v.n

            def ev(sc, par=par, w=c_v.n):
                sc.wait_ge(c_v.sem, w)
                sc.drain()
                sc.activation(ln2d[:, par, 15:16], ln2d[:, par, 14:15],
                              AF.Sqrt).then_inc(c_s.sem, 1)
            p.on("scalar", ev)
            c_s.n += 1
            ln2_sq[(bi, tt)] = c_s.n

        def ln2_B(bi, tt):
            par = tt % 2

            def ev(v, par=par, w=ln2_sq[(bi, tt)]):
                v.wait_ge(c_s.sem, w)
                v.reciprocal(ln2d[:, par, 16:17], ln2d[:, par, 15:16])
                v.drain()
                v.tensor_scalar(ln2d[:, par, 17:18], ln2d[:, par, 12:13],
                                ln2d[:, par, 16:17], -1.0, ALU.mult,
                                ALU.mult).then_inc(c_v.sem, 1)
            p.on("vector", ev)
            c_v.n += 1
            ln2_rc[(bi, tt)] = c_v.n

            def ev(sc, par=par, w=c_v.n, tg=ln2_T_prev[0]):
                sc.wait_ge(c_v.sem, w)
                if tg is not None:
                    sc.wait_ge(c_pe.sem, tg)
                sc.drain()
                sc.activation(xn2tm[:], x1rd[:, :], AF.Identity,
                              bias=ln2d[:, par, 17:18],
                              scale=ln2d[:, par, 16:17]).then_inc(c_s.sem, 1)
            p.on("scalar", ev)
            c_s.n += 1
            ln2_ap[(bi, tt)] = c_s.n
            ln2_ap_last[0] = c_s.n

        def ln2_C1(bi, tt):
            def ev(t, w=ln2_ap[(bi, tt)], g=ln2_pT_guard[0]):
                t.wait_ge(c_s.sem, w)
                if g is not None:
                    t.wait_ge(c_v.sem, g)
                for c in range(NDC):
                    ins = t.transpose(pT[:, c * 128:(c + 1) * 128],
                                      xn2tm[:, c * 128:(c + 1) * 128], identb[:])
                ins.then_inc(c_pe.sem, 1)
            p.on("tensor", ev)
            c_pe.n += 1
            ln2_T_prev[0] = c_pe.n
            ln2_tr[(bi, tt)] = c_pe.n

        def ln2_C2(bi, tt):
            def ev(v, bi=bi, tt=tt, w=ln2_tr[(bi, tt)]):
                v.wait_ge(c_pe.sem, w)
                v.tensor_copy(xn2[:, bi % 2, :, tt * 128:(tt + 1) * 128],
                              pT[:].rearrange("q (c w) -> q c w", w=128)
                              ).then_inc(c_v.sem, 1)
            p.on("vector", ev)
            c_v.n += 1
            ln2_pT_guard[0] = c_v.n
            return c_v.n

        e1_hist = {}
        e5_hist = {}
        e6_hist = {}
        e8_hist = {}

        def act_E012(h):
            bi, hm = divmod(h, 32)
            hb = hm % 2
            i = h % 3
            i4 = h % 4
            grp = hm // 2
            sl = grp % 2

            def ev(t, bi=bi, hm=hm, hb=hb, sl=sl,
                   xd=xn2_ready[bi], wl=(w1load[(bi, grp)] if hm % 2 == 0 else None),
                   ag=e1_hist.get(h - 2)):
                t.wait_ge(c_v.sem, xd)
                if wl is not None:
                    t.wait_ge(c_lw[sl].sem, wl)
                if ag is not None:
                    t.wait_ge(c_s.sem, ag)
                for dc in range(NDC):
                    ins = t.matmul(pA[:, hb * 512:(hb + 1) * 512],
                                   w1s[:, sl, dc, (hm % 2) * 128:(hm % 2 + 1) * 128],
                                   xn2[:, bi % 2, dc, :],
                                   start=(dc == 0), stop=(dc == NDC - 1))
                ins.then_inc(c_pe.sem, 1)
            p.on("tensor", ev)
            c_pe.n += 1
            w1mm[(bi, grp)] = c_pe.n
            w1_pe = c_pe.n

            def ev(sc, hm=hm, hb=hb, i4=i4, w=w1_pe, vg=e6_hist.get(h - 4)):
                sc.wait_ge(c_pe.sem, w)
                if vg is not None:
                    sc.wait_ge(c_v.sem, vg)
                sc.activation(ft[:, i4, :], pA[:, hb * 512:(hb + 1) * 512], AF.Identity,
                              bias=b256[:, hm:hm + 1], scale=256.0).then_inc(c_s.sem, 1)
            p.on("scalar", ev)
            c_s.n += 1
            e1_hist[h] = c_s.n
            sid_done = c_s.n

            def ev(v, i=i, i4=i4, w=sid_done, sg=e5_hist.get(h - 3)):
                v.wait_ge(c_s.sem, w)
                if sg is not None:
                    v.wait_ge(c_s.sem, sg)
                v.tensor_scalar(fki[:, i4, :], ft[:, i4, :], -1152.5, 1152.5,
                                ALU.max, ALU.min)
                v.drain()
                v.tensor_scalar(ft[:, i4, :], fki[:, i4, :], MC, -MC,
                                ALU.add, ALU.add).then_inc(c_v.sem, 1)
            p.on("vector", ev)
            c_v.n += 1
            e1_hist[(h, "va")] = c_v.n

        def act_E34(h):
            i = h % 3
            i4 = h % 4

            def ev(sc, i=i, i4=i4, w=e1_hist[(h, "va")]):
                sc.wait_ge(c_v.sem, w)
                sc.activation(fu5[:, i, :], ft[:, i4, :], AF.Identity,
                              bias=p155[:, 0:1], scale=1.0 / 64.0).then_inc(c_s.sem, 1)
            p.on("scalar", ev)
            c_s.n += 1
            u5_done = c_s.n

            def ev(v, i=i, i4=i4, w=u5_done):
                v.wait_ge(c_s.sem, w)
                v.tensor_scalar(fki[:, i4, :], fu5[:, i, :], MC, -MC, ALU.add, ALU.add)
                v.drain()
                v.scalar_tensor_tensor(ft[:, i4, :], fki[:, i4, :], -1.0, fu5[:, i, :],
                                       ALU.mult, ALU.add).then_inc(c_v.sem, 1)
            p.on("vector", ev)
            c_v.n += 1
            e1_hist[(h, "vb")] = c_v.n

        def act_E56(h):
            i = h % 3
            i4 = h % 4
            j = h % 2

            def ev(sc, i=i, i4=i4, j=j, w=e1_hist[(h, "vb")], fg=e6_hist.get(h - 2)):
                sc.wait_ge(c_v.sem, w)
                if fg is not None:
                    sc.wait_ge(c_v.sem, fg)
                sc.drain()
                sc.add(fu5[:, i, :], ft[:, i4, :], phf[:, 0:1])
                sc.activation(fgl[:, j, :], fki[:, i4, :], AF.Gelu, bias=neg4[:, 0:1],
                              scale=0.25)
                sc.activation(fji[:, j, :], fki[:, i4, :], AF.Gelu, bias=neg375[:, 0:1],
                              scale=0.25).then_inc(c_s.sem, 1)
            p.on("scalar", ev)
            c_s.n += 1
            e5_hist[h] = c_s.n
            s3_done = c_s.n

            def ev(v, i=i, i4=i4, j=j, w=s3_done):
                v.wait_ge(c_s.sem, w)
                v.tensor_sub(ft[:, i4, :], fji[:, j, :], fgl[:, j, :])
                v.drain()
                v.scalar_tensor_tensor(fki[:, i4, :], ft[:, i4, :], 256.0, fu5[:, i, :],
                                       ALU.mult, ALU.mult)
                v.drain()
                v.scalar_tensor_tensor(fu5[:, i, :], fgl[:, j, :], 256.0, fki[:, i4, :],
                                       ALU.mult, ALU.add).then_inc(c_v.sem, 1)
            p.on("vector", ev)
            c_v.n += 1
            e6_hist[h] = c_v.n

        def act_E78(h):
            bi, hm = divmod(h, 32)
            i = h % 3
            j = h % 2

            def ev(sc, bi=bi, hm=hm, i=i, j=j, w=e6_hist[h],
                   gg=w2_last_of.get(bi - 2)):
                sc.wait_ge(c_v.sem, w)
                if gg is not None:
                    sc.wait_ge(c_pe.sem, gg)
                sc.add(fji[:, j, :], fu5[:, i, :], cmc[:, 0:1])
                sc.drain()
                sc.activation(gbuf[:, bi % 2, hm, :], fji[:, j, :], AF.Identity,
                              bias=nmc256[:, 0:1],
                              scale=1.0 / 256.0).then_inc(c_s.sem, 1)
            p.on("scalar", ev)
            c_s.n += 1
            e8_hist[h] = c_s.n

        w2_pe_of = {}

        def w2_mm(bi, j):
            s, tb = blocks[bi]
            tt, nb = j // 2, j % 2
            ob = j % 2
            par = tt % 2
            if nb == 0:
                def ev(sy, s=s, tb=tb, tt=tt, par=par, g=x2rd_guard.get(par)):
                    if g is not None:
                        sy.wait_ge(c_v.sem, g)
                    sy.dma_start(out=x1rd2[:, par, :],
                                 in_=x1s[s, tb * TB + tt * 128:tb * TB + (tt + 1) * 128, :]
                                 ).then_inc(s_x[2 + par].sem, 16)
                p.on("sync", ev)
                s_x[2 + par].n += 16

            def ev(t, bi=bi, tt=tt, nb=nb, ob=ob, gd=g_done_of[bi],
                   g=w2ev_guard.get(ob), fw=first_w2[0]):
                if fw:
                    t.wait_ge(c_lb.sem, ldw_B)
                t.wait_ge(c_s.sem, gd)
                if g is not None:
                    t.wait_ge(c_v.sem, g)
                for hm in range(NHM):
                    ins = t.matmul(pA[:, 1024 + ob * 512:1024 + (ob + 1) * 512],
                                   gbuf[:, bi % 2, hm, tt * 128:(tt + 1) * 128],
                                   w2b[:, hm, nb * 512:(nb + 1) * 512],
                                   start=(hm == 0), stop=(hm == NHM - 1))
                ins.then_inc(c_pe.sem, 1)
            p.on("tensor", ev)
            c_pe.n += 1
            first_w2[0] = False
            w2_pe_of[(bi, j)] = c_pe.n
            w2_last_of[bi] = c_pe.n

        def w2_ev(bi, j):
            s, tb = blocks[bi]
            tt, nb = j // 2, j % 2
            ob = j % 2
            par = tt % 2
            x2ld = s_x[2 + par].n

            def ev(v, nb=nb, ob=ob, par=par, w=w2_pe_of[(bi, j)], x2=x2ld,
                   og=(c_st.n if c_st.n > 0 else None)):
                v.wait_ge(c_pe.sem, w)
                v.wait_ge(s_x[2 + par].sem, x2)
                if og is not None:
                    v.wait_ge(c_st.sem, og)
                v.scalar_tensor_tensor(w2scr[:, 0, :],
                                       pA[:, 1024 + ob * 512:1024 + (ob + 1) * 512],
                                       256.0, b2bc[:, nb * 512:(nb + 1) * 512],
                                       ALU.mult, ALU.add)
                v.drain()
                v.tensor_scalar(w2scr[:, 1, :], w2scr[:, 0, :], MC, -MC, ALU.add, ALU.add)
                v.drain()
                v.scalar_tensor_tensor(outt[:, ob, :], w2scr[:, 1, :], 1.0 / 256.0,
                                       x1rd2[:, par, nb * 512:(nb + 1) * 512],
                                       ALU.mult, ALU.add).then_inc(c_v.sem, 1)
            p.on("vector", ev)
            c_v.n += 1
            w2ev_guard[ob] = c_v.n
            if nb == 1:
                x2rd_guard[par] = c_v.n

            def ev(sy, s=s, tb=tb, tt=tt, nb=nb, ob=ob, w=c_v.n):
                sy.wait_ge(c_v.sem, w)
                sy.dma_start(out=out[s, tb * TB + tt * 128:tb * TB + (tt + 1) * 128,
                                     nb * 512:(nb + 1) * 512],
                             in_=outt[:, ob, :]).then_inc(c_st.sem, 16)
            p.on("sync", ev)
            c_st.n += 16

        w1_group_load(0, 0)
        w1_group_load(0, 1)
        for tt in range(4):
            ln2_L(0, tt)
            ln2_A(0, tt)
            ln2_B(0, tt)
            ln2_C1(0, tt)
            xn2_ready[0] = ln2_C2(0, tt)

        w2mm_marks = (3, 6, 9, 12, 15, 18, 21, 24)
        w2ev_marks = (5, 8, 11, 14, 17, 20, 23, 26)
        NG = NB * NHM
        for g in range(NG + 3):
            if g < NG:
                bi, hm = divmod(g, NHM)
                if bi > 0 and hm in w2mm_marks:
                    w2_mm(bi - 1, w2mm_marks.index(hm))
                if bi > 0 and hm in w2ev_marks:
                    w2_ev(bi - 1, w2ev_marks.index(hm))
                if hm % 2 == 0 and 2 <= hm // 2 + 1 <= 15:
                    w1_group_load(bi, hm // 2 + 1)
                if hm == 30 and bi + 1 < NB:
                    w1_group_load(bi + 1, 0)
                act_E012(g)
                if hm == 31 and bi + 1 < NB:
                    w1_group_load(bi + 1, 1)
                if bi + 1 < NB and 3 <= hm <= 20:
                    tt, st = divmod(hm - 3, 4)
                    if st == 0 and hm >= 3 and tt < 4:
                        ln2_L(bi + 1, tt)
                    elif st == 2:
                        ln2_A(bi + 1, tt)
                    elif st == 3:
                        ln2_B(bi + 1, tt)
                    elif st == 1 and tt >= 1:
                        ln2_C1(bi + 1, tt - 1)
                        xn2_ready[bi + 1] = ln2_C2(bi + 1, tt - 1)
            if g >= 1 and g - 1 < NG:
                act_E34(g - 1)
            if g >= 2 and g - 2 < NG:
                act_E56(g - 2)
            if g >= 3 and g - 3 < NG:
                act_E78(g - 3)
                if (g - 3) % NHM == NHM - 1:
                    g_done_of[(g - 3) // NHM] = c_s.n
        w2_mm(NB - 1, 0)
        for j in range(1, 8):
            w2_mm(NB - 1, j)
            w2_ev(NB - 1, j - 1)
        w2_ev(NB - 1, 7)
        st_final = c_st.n

        def ev(sy, w=st_final):
            sy.wait_ge(c_st.sem, w)
        p.on("sync", ev)

        esB.close()

        block = es.enter_context(nc.Block())

        @block.sync
        def _(sync):
            p.run("sync", sync)

        @block.gpsimd
        def _(g):
            p.run("gpsimd", g)

        @block.tensor
        def _(t):
            p.run("tensor", t)

        @block.vector
        def _(v):
            p.run("vector", v)

        @block.scalar
        def _(sc):
            p.run("scalar", sc)

    return nc


def kernel(L0_attn_x, L0_attn_w_qkv, L0_attn_b_qkv, L0_attn_w_proj, L0_attn_b_proj,
           L0_ffn_w1, L0_ffn_b1, L0_ffn_w2, L0_ffn_b2):
    xs = np.ascontiguousarray(np.asarray(L0_attn_x, dtype=np.float32))
    common = {
        "wqkv": np.ascontiguousarray(np.asarray(L0_attn_w_qkv, dtype=np.float32)),
        "bqkv": np.ascontiguousarray(np.asarray(L0_attn_b_qkv, dtype=np.float32)),
        "wproj": np.ascontiguousarray(np.asarray(L0_attn_w_proj, dtype=np.float32)),
        "bproj": np.ascontiguousarray(np.asarray(L0_attn_b_proj, dtype=np.float32)),
        "w1": np.ascontiguousarray(np.asarray(L0_ffn_w1, dtype=np.float32)),
        "b1": np.ascontiguousarray(np.asarray(L0_ffn_b1, dtype=np.float32)),
        "w2": np.ascontiguousarray(np.asarray(L0_ffn_w2, dtype=np.float32)),
        "b2": np.ascontiguousarray(np.asarray(L0_ffn_b2, dtype=np.float32)),
        "ident": np.eye(128, dtype=np.float32),
    }
    nc = build()
    in_maps = []
    for c in range(8):
        m = dict(common)
        m["x"] = np.ascontiguousarray(xs[2 * c:2 * c + 2])
        in_maps.append(m)
    res = run_bass_kernel_spmd(nc, in_maps, core_ids=list(range(8)))
    return np.concatenate([r["out"] for r in res.results], axis=0)
